# version 18
# baseline (speedup 1.0000x reference)
import contextlib
import numpy as np
import ml_dtypes
import concourse.bass as bass
import concourse.mybir as mybir
from concourse.bass_utils import run_bass_kernel_spmd

F32 = mybir.dt.float32
BF16 = mybir.dt.bfloat16
I32 = mybir.dt.int32
AF = mybir.ActivationFunctionType
ALU = mybir.AluOpType
AX = mybir.AxisListType

E = 1024
H_HEADS = 16
KVH = 4
D = 64
IC = 512
SCALE = 0.125
DEQ = float(np.float32(1.0 / 127.0))

SEM_OF = {"sync": "SY", "gpsimd": "GP", "tensor": "T", "vector": "V", "scalar": "SC"}
DMA_QUEUE_OF = {"sync": "DS", "gpsimd": "DG"}

HEAD_OF_POS = [0, 4, 1, 5, 2, 6, 3, 7, 8, 12, 9, 13, 10, 14, 11, 15]

DEBUG_DUMPS = False
DEBUG_LN = False


class Cfg:
    def __init__(self, S=4096, HID=3072):
        self.S = S
        self.HID = HID
        self.NT = S // 128
        self.NIC = S // IC
        self.TPC = IC // 128
        self.NHD = HID // 128


DMA_LANES = {"DS": 8, "DG": 4, "OS": 2}


class Ctx:
    def __init__(self, record, engine=None, eng=None, events=None, sems=None):
        self.record = record
        self.engine = engine
        self.eng = eng
        self.events = events if events is not None else {}
        self.sems = sems
        self.counts = {}
        self.qpos = {q: 0 for q in DMA_LANES}
        self.hwm = {}
        self.trace_log = []

    def _bump(self, sem_name, inc):
        self.counts[sem_name] = self.counts.get(sem_name, 0) + inc
        return self.counts[sem_name]

    def _wait(self, sem_name, v):
        if self.hwm.get(sem_name, 0) < v:
            self.eng.wait_ge(self.sems[sem_name], v)
            self.hwm[sem_name] = v

    def op(self, engine, label, waits, emit, inc=1):
        sem_name = SEM_OF[engine]
        if self.record:
            if label is not None:
                assert label not in self.events, f"dup label {label}"
                self.events[label] = (sem_name, self._bump(sem_name, inc))
            self.trace_log.append((engine, list(waits), label,
                                   sem_name if label is not None else None,
                                   inc, None))
            return
        if engine != self.engine:
            return
        for w in waits:
            if w is None:
                continue
            s, v = self.events[w]
            self._wait(s, v)
        ins = emit(self.eng)
        if label is not None:
            ins.then_inc(self.sems[sem_name], inc)

    def dma(self, engine, label, waits, emit, sem=None):
        q = sem or DMA_QUEUE_OF[engine]
        lane = self.qpos[q] % DMA_LANES[q]
        self.qpos[q] += 1
        sem_name = f"{q}{lane}"
        if self.record:
            v = self._bump(sem_name, 16)
            if label is not None:
                assert label not in self.events, f"dup label {label}"
                self.events[label] = (sem_name, v)
            self.trace_log.append((engine, list(waits), label, sem_name, 16,
                                   (sem_name, v - 16) if v > 16 else None))
            return
        prev = self.counts.get(sem_name, 0)
        self.counts[sem_name] = prev + 16
        if engine != self.engine:
            return
        for w in waits:
            if w is None:
                continue
            s, v = self.events[w]
            self._wait(s, v)
        if prev > 0:
            self._wait(sem_name, prev)
        ins = emit(self.eng)
        ins.then_inc(self.sems[sem_name], 16)

    def wait_only(self, engine, waits):
        self.op(engine, None, waits, lambda e: e.nop())


ALLOC_LOG = []


class SBAlloc:
    def __init__(self, nc, base=16512, top=229344):
        self.nc = nc
        self.cur = base
        self.top = top

    def alloc(self, name, shape, dtype):
        sz = {F32: 4, BF16: 2, I32: 4}[dtype]
        nb = sz
        for d_ in shape[1:]:
            nb *= d_
        off = (self.cur + 31) // 32 * 32
        self.cur = off + nb
        assert self.cur <= self.top, f"SBUF overflow at {name}: {self.cur}"
        ALLOC_LOG.append((name, off, self.cur, shape[0]))
        return self.nc.alloc_sbuf_tensor_at(name, list(shape), dtype, offset=off)

    def mark(self):
        return self.cur

    def reset(self, m):
        self.cur = m


def host_constants(S):
    inv = 1.0 / (10000.0 ** (np.arange(0, D, 2, dtype=np.float32) / D))
    t = np.arange(S, dtype=np.float32)
    fr = np.outer(t, inv)
    cos = np.cos(fr).astype(np.float32)
    sin = np.sin(fr).astype(np.float32)
    ident = np.eye(128, dtype=np.float32)
    ones = np.ones((1, 128), dtype=np.float32)
    masks = np.zeros((128, 2048), dtype=np.float32)
    jj = np.arange(128)[:, None]
    ii = np.arange(512)[None, :]
    for m in range(4):
        masks[:, m * 512:(m + 1) * 512] = (ii >= 128 * m + jj)
    perm_q = np.zeros(1024, np.int64)
    for pos in range(16):
        h = HEAD_OF_POS[pos]
        for tt in range(32):
            perm_q[pos * 64 + tt] = h * 64 + 2 * tt
            perm_q[pos * 64 + 32 + tt] = h * 64 + 2 * tt + 1
    perm_k = np.zeros(256, np.int64)
    perm_v = np.zeros(256, np.int64)
    for g in range(4):
        for tt in range(32):
            perm_k[g * 64 + tt] = g * 128 + 4 * tt
            perm_k[g * 64 + 32 + tt] = g * 128 + 4 * tt + 2
        for d_ in range(64):
            perm_v[g * 64 + d_] = g * 128 + 2 * d_ + 1
    return cos, sin, ident, ones, masks, perm_q, perm_k, perm_v


def build_nc(cfg):
    S, HID, NT, NIC, TPC, NHD = cfg.S, cfg.HID, cfg.NT, cfg.NIC, cfg.TPC, cfg.NHD
    nc = bass.Bass("TRN2", target_bir_lowering=False)

    dx = nc.dram_tensor("x", [S, E], F32, kind="ExternalInput").ap()
    dwqa = nc.dram_tensor("wqa", [E, 64], F32, kind="ExternalInput").ap()
    dwqb = nc.dram_tensor("wqb", [64, E], F32, kind="ExternalInput").ap()
    dwkva = nc.dram_tensor("wkva", [E, 32], F32, kind="ExternalInput").ap()
    dwkvk = nc.dram_tensor("wkvk", [32, 256], F32, kind="ExternalInput").ap()
    dwkvv = nc.dram_tensor("wkvv", [32, 256], F32, kind="ExternalInput").ap()
    dwoutp = nc.dram_tensor("woutp", [E, E], F32, kind="ExternalInput").ap()
    dbout = nc.dram_tensor("bout", [1, E], F32, kind="ExternalInput").ap()
    dwg = nc.dram_tensor("wg", [HID, E], I32, kind="ExternalInput").ap()
    dwu = nc.dram_tensor("wu", [HID, E], I32, kind="ExternalInput").ap()
    dwd = nc.dram_tensor("wd", [E, HID], I32, kind="ExternalInput").ap()
    dcos = nc.dram_tensor("cost", [S, 32], F32, kind="ExternalInput").ap()
    dsin = nc.dram_tensor("sint", [S, 32], F32, kind="ExternalInput").ap()
    dident = nc.dram_tensor("ident", [128, 128], F32, kind="ExternalInput").ap()
    dones = nc.dram_tensor("ones", [1, 128], F32, kind="ExternalInput").ap()
    dmasks = nc.dram_tensor("masks", [128, 2048], F32, kind="ExternalInput").ap()
    dout = nc.dram_tensor("out", [S, E], F32, kind="ExternalOutput").ap()
    dxnt = nc.dram_tensor("xnt_s", [E, S], BF16, kind="Internal").ap()
    drecl = nc.dram_tensor("recl_s", [1, 1024], F32, kind="Internal").ap()
    dwgt = nc.dram_tensor("wgt_s", [NHD * 128, E], BF16, kind="Internal").ap()
    dwut = nc.dram_tensor("wut_s", [NHD * 128, E], BF16, kind="Internal").ap()
    dwdt = nc.dram_tensor("wdt_s", [NHD * 128, E], BF16, kind="Internal").ap()

    A = SBAlloc(nc)
    kT0 = A.alloc("kT0", [128, S], BF16)
    kT1 = A.alloc("kT1", [128, S], BF16)
    Vp = A.alloc("Vp", [128, KVH * NT * 65], BF16)
    qk1T = A.alloc("qk1T", [128, S], BF16)
    cosb = A.alloc("cosb", [128, NT * 32], F32)
    sinb = A.alloc("sinb", [128, NT * 32], F32)
    identb = A.alloc("identb", [128, 128], BF16)
    onesb = A.alloc("onesb", [1, 128], BF16)
    masksb = A.alloc("masksb", [128, 2048], BF16)
    wqa_b = A.alloc("wqa_b", [128, 512], BF16)
    wqb_b = A.alloc("wqb_b", [64, E], BF16)
    wkva_b = A.alloc("wkva_b", [128, 256], BF16)
    wkvk_b = A.alloc("wkvk_b", [128, 256], BF16)
    wkvv_b = A.alloc("wkvv_b", [128, 256], BF16)
    wout_pair = A.alloc("wout_pair", [128, 8 * E], BF16)
    bout_b = A.alloc("bout_b", [1, E], BF16)
    kn2max = A.alloc("kn2max", [128, 1], F32)
    kn2 = [A.alloc(f"kn2_{b}", [128, 1], F32) for b in range(2)]
    kn2m_bf = A.alloc("kn2m_bf", [128, 8], BF16)
    kmax2 = A.alloc("kmax2", [1, 1], F32)
    biasC = A.alloc("biasC", [128, 1], F32)
    qn2 = [A.alloc(f"qn2_{b}", [128, 1], F32) for b in range(2)]
    qn2m = A.alloc("qn2m", [128, 1], F32)
    qn2m_bf = A.alloc("qn2m_bf", [128, 8], BF16)
    c1 = A.alloc("c1", [1, 1], F32)
    c2 = A.alloc("c2", [1, 1], F32)
    c3 = A.alloc("c3", [1, 1], F32)
    c4 = A.alloc("c4", [1, 1], F32)
    eps_b = A.alloc("eps_b", [128, 1], F32)
    qrow_sb = A.alloc("qrow_sb", [1, 128], F32)
    krow_sb = A.alloc("krow_sb", [1, 128], F32)
    negC_bf = A.alloc("negC_bf", [1, 1], BF16)
    rt1 = A.alloc("rt1", [128, 512], F32)
    rt2 = A.alloc("rt2", [128, 512], F32)

    scratch0 = A.mark()
    xnT = [A.alloc(f"xnT{k}", [128, S], BF16) for k in range(8)]
    xbuf = [A.alloc(f"xbuf{b}", [128, E], F32) for b in range(2)]
    xn_tile = [A.alloc(f"xn_tile{b}", [128, E], BF16) for b in range(2)]
    xsq = A.alloc("xsq", [128, E], F32)
    lnsum = A.alloc("lnsum", [128, 1], F32)
    sq2 = A.alloc("sq2", [128, 1], F32)
    negm = A.alloc("negm", [128, 1], F32)
    m2v = A.alloc("m2v", [128, 1], F32)
    varv = A.alloc("varv", [128, 1], F32)
    lnv = A.alloc("lnv", [128, 1], F32)
    rstd = A.alloc("rstd", [128, 1], F32)
    kroped = [A.alloc(f"kroped{b}", [128, 256], BF16) for b in range(2)]
    ksq = A.alloc("ksq", [128, 256], BF16)
    wnat = [A.alloc(f"wnat{b}", [128, E], BF16) for b in range(4)]
    wstage = [A.alloc(f"wstage{b}", [128, E], BF16) for b in range(2)]
    wd_bf = [A.alloc(f"wd_bf{q}", [128, HID], BF16) for q in range(4)]
    pro_end = A.cur

    A.reset(scratch0)
    xnS = [A.alloc("xnS0", [128, 8 * 512], BF16)]
    wgB = [A.alloc(f"wgB{b}", [128, E], BF16) for b in range(2)]
    wuB = [A.alloc(f"wuB{b}", [128, E], BF16) for b in range(2)]
    wdB = [A.alloc(f"wdB{b}", [128, 2 * E], BF16) for b in range(2)]
    hT = A.alloc("hT", [128, NHD * 512], BF16)
    qroped = [A.alloc(f"qroped{b}", [128, E], BF16) for b in range(2)]
    qTc = [A.alloc(f"qTc{k}", [128, IC], BF16) for k in range(8)]
    Pt = [A.alloc(f"Pt{b}", [128, 1024], BF16) for b in range(4)]
    ot = [A.alloc(f"otp{p}", [128, IC], BF16) for p in range(8)]
    ot_stg = A.alloc("ot_stg", [64, IC], BF16)
    g_nat = [A.alloc(f"g_nat{t}", [128, E], BF16) for t in range(2 * TPC)]
    eF = [A.alloc(f"eF{b}", [128, 512], F32) for b in range(2)]
    eG = [A.alloc(f"eG{b}", [128, E], F32) for b in range(2)]
    gB2 = A.alloc("gB2", [128, 512], F32)
    jnk = A.alloc("jnk", [128, 8], F32)
    outb = [A.alloc(f"outb{b}", [128, E], F32) for b in range(2)]
    _u0 = A.mark()
    sgtmp = A.alloc("sgtmp", [128, E], F32)
    _u1 = A.mark()
    A.reset(_u0)
    recl = A.alloc("recl", [128, 256], F32)
    A.reset(_u0)
    qsq = A.alloc("qsq", [128, E], BF16)
    A.reset(_u1)
    Of32 = A.alloc("Of32", [64, 1024], F32)
    rec_rep = A.alloc("rec_rep", [64, 1024], F32)
    main_end = A.cur

    P = [nc.alloc_psum_tensor(f"P{i}", [128, 1024], F32) for i in range(2)]
    _p2_base = nc.psum_base
    P.append(nc.alloc_psum_tensor("P2", [128, 1024], F32))
    _after_p2 = nc.psum_base
    nc.psum_base = _p2_base
    QW = nc.alloc_psum_tensor("QW", [128, 1024], BF16)
    nc.psum_base = _after_p2
    _qx_base = nc.psum_base
    QX = nc.alloc_psum_tensor("QX", [128, 1024], BF16)
    QX2 = nc.alloc_psum_tensor("QX2", [128, 1024], BF16)
    nc.psum_base = _qx_base
    P3 = nc.alloc_psum_tensor("P3", [128, 1024], F32)

    dump_list = []
    dump_drams = {}
    FULLDBG = DEBUG_DUMPS is True
    if DEBUG_DUMPS:
        if FULLDBG:
            dbg1 = A.alloc("dbg1", [128, 1024], F32)
            dbg2 = A.alloc("dbg2", [128, 1024], F32)
            dbg3 = A.alloc("dbg3", [128, 1024], BF16)
        else:
            dbg1 = dbg2 = dbg3 = None
        dbg4 = A.alloc("dbg4", [128, 1], F32)
        dbg5 = A.alloc("dbg5", [1, 8], F32)
        dbg6 = A.alloc("dbg6", [1, 128], F32)
        build_nc.dbg = (dbg1, dbg2, dbg3, dbg4, dbg5, dbg6)
        def dreg(nm, sb_ap, shape, dtype):
            dump_drams[nm] = nc.dram_tensor(nm, shape, dtype, kind="ExternalOutput").ap()
            dump_list.append((nm, sb_ap))
        if FULLDBG:
            dreg("d_qk1T", qk1T[:, :], [128, S], BF16)
            dreg("d_kT0", kT0[:, :], [128, S], BF16)
            dreg("d_kT1", kT1[:, :], [128, S], BF16)
            dreg("d_Vp", Vp[:, :], [128, KVH * NT * 65], BF16)
            for k in range(8):
                dreg(f"d_qTc{k}", qTc[k][:, :], [128, IC], BF16)
        for hh in range(8):
            dreg(f"d_ot{hh}", ot[hh][:, :], [128, IC], BF16)
        dreg("d_xnS", xnS[0][:, :], [128, 8 * 512], BF16)
        if not FULLDBG:
            dreg("d_xnt_full", dxnt, [E, S], BF16)
            dreg("d_xin", dx, [S, E], F32)
            dreg("d_qTc0", qTc[0][:, :], [128, IC], BF16)
            dreg("d_qTc7", qTc[7][:, :], [128, IC], BF16)
            dreg("d_kT0", kT0[:, :], [128, S], BF16)
            dreg("d_Vp", Vp[:, :], [128, KVH * NT * 65], BF16)
            dreg("d_qk1T", qk1T[:, :], [128, S], BF16)
        dreg("d_hT", hT[:, :], [128, NHD * 512], BF16)
        for t in range(TPC):
            dreg(f"d_gnat{t}", g_nat[t][:, :], [128, E], F32)
        dreg("d_recrep", rec_rep[:, :], [64, 1024], BF16)
        dreg("d_reclbf", recl[:, :], [1, 1024], F32)
        dreg("d_Pt0", Pt[0][:, :], [128, 1024], BF16)
        dreg("d_Pt1", Pt[1][:, :], [128, 1024], BF16)
        if FULLDBG:
            dreg("d_dbg1", dbg1[:, :], [128, 1024], F32)
            dreg("d_dbg2", dbg2[:, :], [128, 1024], F32)
            dreg("d_dbg3", dbg3[:, :], [128, 1024], BF16)
        dreg("d_xnt_s", dxnt, [E, S], BF16)
        dreg("d_wgt_s", dwgt, [NHD * 128, E], BF16)
        dreg("d_wut_s", dwut, [NHD * 128, E], BF16)
        dreg("d_wdt_s", dwdt, [NHD * 128, E], BF16)

    lndbg = {}
    if DEBUG_LN:
        lndbg["g_x0"] = nc.dram_tensor("g_x0", [128, E], F32, kind="ExternalOutput").ap()
        lndbg["g_xn0"] = nc.dram_tensor("g_xn0", [128, E], BF16, kind="ExternalOutput").ap()
        for nm in ("g_lnsum", "g_sq2", "g_negm", "g_varv", "g_lnv", "g_rstd", "g_m2v"):
            lndbg[nm] = nc.dram_tensor(nm, [128, 1], F32, kind="ExternalOutput").ap()

    sems = {}
    sem_guards = []
    sem_names = ["T", "V", "SC", "SY", "GP"]
    for q, n in DMA_LANES.items():
        sem_names += [f"{q}{i}" for i in range(n)]
    for name in sem_names:
        g = nc.semaphore(name)
        h = g.__enter__()
        sem_guards.append(g)
        sems[name] = h

    def program(ctx):
        op, dma = ctx.op, ctx.dma
        TT = ALU

        def cast_load(name, out_ap, in_ap):
            dma("gpsimd", ("c", name), [], lambda e, o=out_ap, i=in_ap: e.dma_start(o, i))

        cast_load("ident", identb[:, :], dident)
        cast_load("ones", onesb[:, :], dones)
        cast_load("masks", masksb[:, :], dmasks)
        cast_load("wqa", wqa_b[:, :].rearrange("p (c k) -> p c k", k=64),
                  dwqa.rearrange("(c p) k -> p c k", p=128))
        cast_load("wqb", wqb_b[:, :], dwqb)
        cast_load("wkva", wkva_b[:, :].rearrange("p (c k) -> p c k", k=32),
                  dwkva.rearrange("(c p) k -> p c k", p=128))
        cast_load("wkvk", wkvk_b[64:96, :], dwkvk)
        cast_load("wkvv", wkvv_b[64:96, :], dwkvv)
        cast_load("wout", wout_pair[:, :].rearrange("p (c n) -> p c n", n=E),
                  dwoutp.rearrange("(c p) n -> p c n", p=128))
        cast_load("bout", bout_b[:, :], dbout)
        dma("sync", ("c", "cos"), [], lambda e: e.dma_start(
            cosb[:, :].rearrange("p (c k) -> p c k", k=32),
            dcos.rearrange("(c p) k -> p c k", p=128)))
        dma("sync", ("c", "sin"), [], lambda e: e.dma_start(
            sinb[:, :].rearrange("p (c k) -> p c k", k=32),
            dsin.rearrange("(c p) k -> p c k", p=128)))
        op("vector", ("vpms",), [], lambda e: e.memset(Vp[:, :], 1.0))
        op("vector", ("epsms",), [], lambda e: e.memset(eps_b[:, :], 1e-5))

        cvt_units = []

        def mk_w_unit(tag, src_ap, dst_ap, Hh, prev_wt, prev_wc, prev_ws):
            def dmau():
                b = Hh % 4
                dma("gpsimd", ("wl", tag, Hh),
                    [("wt", tag, Hh - 4) if Hh >= 4 else prev_wt],
                    lambda e: e.dma_start(
                        wnat[b][:, :], src_ap[Hh * 128:(Hh + 1) * 128, :]))

            def peu():
                b = Hh % 4
                for pchunk in range(8):
                    op("tensor", ("wt", tag, Hh) if pchunk == 7 else None,
                       [("wl", tag, Hh),
                        ("wc", tag, Hh - 1) if Hh >= 1 else prev_wc],
                       lambda e, pchunk=pchunk: e.transpose(
                           QW[:, pchunk * 128:(pchunk + 1) * 128],
                           wnat[b][:, pchunk * 128:(pchunk + 1) * 128], identb[:, :]))
                op("scalar", ("wc", tag, Hh),
                   [("wt", tag, Hh), ("ws", tag, Hh - 2) if Hh >= 2 else prev_ws],
                   lambda e: e.copy(wstage[Hh % 2][:, :], QW[:, :]))
                dma("sync", ("ws", tag, Hh), [("wc", tag, Hh)],
                    lambda e: e.dma_start(
                        dst_ap[Hh * 128:(Hh + 1) * 128, :], wstage[Hh % 2][:, :]))
            return (dmau, peu)

        for Hh in range(NHD):
            cvt_units.append(mk_w_unit("wg", dwg, dwgt, Hh, None, None, None))
        for Hh in range(NHD):
            cvt_units.append(mk_w_unit(
                "wu", dwu, dwut, Hh, ("wt", "wg", NHD - 1),
                ("wc", "wg", NHD - 1), ("ws", "wg", NHD - 1)))

        def mk_wdl_unit(h4):
            def dmau():
                for q in range(2):
                    dma("gpsimd", ("wdl", h4, q),
                        [("wdt", h4 - 2, NHD - 1) if h4 >= 2 else None],
                        lambda e, q=q: e.dma_start(
                            wd_bf[q + 2 * (h4 % 2)][:, :],
                            dwd[(h4 * 2 + q) * 128:(h4 * 2 + q + 1) * 128, :]))
            return (dmau, None)

        def mk_wd_unit(h4, Hh):
            def peu():
                for q in range(2):
                    op("tensor", ("wdt", h4, Hh) if q == 1 else None,
                       [("wdl", h4, 1), ("wc", "wu", NHD - 1),
                        ("wdc", h4, Hh - 1) if Hh >= 1 else (("wdc", h4 - 1, NHD - 1) if h4 >= 1 else None)],
                       lambda e, q=q: e.transpose(
                           QW[:, q * 128:(q + 1) * 128],
                           wd_bf[q + 2 * (h4 % 2)][:, Hh * 128:(Hh + 1) * 128], identb[:, :]))
                wdc_w = [("wdt", h4, Hh)]
                if Hh >= 2:
                    wdc_w.append(("wds", h4, Hh - 2))
                elif h4 >= 1:
                    wdc_w.append(("wds", h4 - 1, NHD - 1))
                else:
                    wdc_w.append(("ws", "wu", NHD - 1))
                op("scalar", ("wdc", h4, Hh), wdc_w,
                   lambda e: e.copy(wstage[Hh % 2][:, 0:256], QW[:, 0:256]))
                dma("sync", ("wds", h4, Hh), [("wdc", h4, Hh)],
                    lambda e: e.dma_start(
                        dwdt[Hh * 128:(Hh + 1) * 128, h4 * 256:(h4 + 1) * 256],
                        wstage[Hh % 2][:, 0:256]))
            return (None, peu)

        for h4 in range(4):
            cvt_units.append(mk_wdl_unit(h4))
            for Hh in range(NHD):
                cvt_units.append(mk_wd_unit(h4, Hh))

        cvt_pos = {"d": 0, "p": 0}

        def pop_cvt(n):
            for _ in range(n):
                if cvt_pos["p"] >= len(cvt_units):
                    return
                while cvt_pos["d"] < min(cvt_pos["p"] + 4, len(cvt_units)):
                    d = cvt_units[cvt_pos["d"]][0]
                    if d is not None:
                        d()
                    cvt_pos["d"] += 1
                pe = cvt_units[cvt_pos["p"]][1]
                if pe is not None:
                    pe()
                cvt_pos["p"] += 1

        for it in range(NT):
            b = it % 2
            dma("sync", ("x", it), [("xn", it - 2) if it >= 2 else ("c", "bout"),
                                    ("gx0",) if (DEBUG_LN and it == 2) else None],
                lambda e, it=it, b=b: e.dma_start(xbuf[b][:, :], dx[it * 128:(it + 1) * 128, :]))
            if DEBUG_LN and it == 0:
                dma("sync", ("gx0",), [("x", 0)],
                    lambda e: e.dma_start(lndbg["g_x0"], xbuf[0][:, :]))
            st_guard = [("gst", k) for k in range(5)] if (DEBUG_LN and it == 1) else []
            op("vector", None, [("x", it)] + st_guard,
               lambda e, b=b: e.reduce_sum(lnsum[:, :], xbuf[b][:, :], axis=AX.X))
            op("vector", None, [],
               lambda e, b=b: e.tensor_mul(xsq[:, :], xbuf[b][:, :], xbuf[b][:, :]))
            op("vector", None, [], lambda e: e.tensor_scalar_mul(negm[:, :], lnsum[:, :], -1.0 / E))
            op("vector", None, [], lambda e: e.reduce_sum(sq2[:, :], xsq[:, :], axis=AX.X))
            op("vector", None, [], lambda e: e.tensor_mul(m2v[:, :], negm[:, :], negm[:, :]))
            op("vector", None, [],
               lambda e: e.scalar_tensor_tensor(varv[:, :], sq2[:, :], 1.0 / E, m2v[:, :], TT.mult, TT.subtract))
            op("vector", None, [],
               lambda e: e.scalar_tensor_tensor(varv[:, :], sq2[:, :], 1.0 / E, m2v[:, :], TT.mult, TT.subtract))
            op("vector", ("var", it), [],
               lambda e: e.scalar_tensor_tensor(varv[:, :], sq2[:, :], 1.0 / E, m2v[:, :], TT.mult, TT.subtract))
            if DEBUG_LN and it == 0:
                for k, (nm, ap) in enumerate((("g_lnsum", lnsum), ("g_sq2", sq2),
                                              ("g_negm", negm), ("g_varv", varv),
                                              ("g_m2v", m2v))):
                    dma("sync", ("gst", k), [("var", 0)],
                        lambda e, nm=nm, ap=ap: e.dma_start(lndbg[nm], ap[:, :]))
            ln_guard = [("gla", 0), ("gla", 1)] if (DEBUG_LN and it == 1) else []
            op("scalar", None, [("var", it), ("epsms",)] + ln_guard, lambda e: e.activation(lnv[:, :], varv[:, :], AF.Ln, bias=eps_b[:, 0:1]))
            op("scalar", None, [], lambda e: e.activation(rstd[:, :], lnv[:, :], AF.Exp, scale=-0.5))
            op("scalar", ("rstd", it), [], lambda e: e.activation(rstd[:, :], lnv[:, :], AF.Exp, scale=-0.5))
            if DEBUG_LN and it == 0:
                for k, (nm, ap) in enumerate((("g_lnv", lnv), ("g_rstd", rstd))):
                    dma("sync", ("gla", k), [("rstd", 0)],
                        lambda e, nm=nm, ap=ap: e.dma_start(lndbg[nm], ap[:, :]))
            op("vector", ("xn", it), [("rstd", it), ("xp", it - 2) if it >= 2 else None,
                                      ("gxn0",) if (DEBUG_LN and it == 2) else None],
               lambda e, b=b: e.tensor_scalar(xn_tile[b][:, :], xbuf[b][:, :], negm[:, :], rstd[:, :], TT.add, TT.mult))
            pop_cvt(1)
            if DEBUG_LN and it == 0:
                dma("sync", ("gxn0",), [("xn", 0)],
                    lambda e: e.dma_start(lndbg["g_xn0"], xn_tile[0][:, :]))
            ps = QX if b == 0 else QX2
            for k in range(8):
                op("tensor", ("xp", it) if k == 7 else None,
                   [("xn", it), ("c", "ident"), ("xncp", it - 2) if it >= 2 else None],
                   lambda e, k=k, b=b, ps=ps: e.transpose(
                       ps[:, k * 128:(k + 1) * 128], xn_tile[b][:, k * 128:(k + 1) * 128], identb[:, :]))
            for k in range(8):
                op("scalar", ("xncp", it) if k == 7 else None, [("xp", it)],
                   lambda e, k=k, it=it, ps=ps: e.copy(xnT[k][:, it * 128:(it + 1) * 128], ps[:, k * 128:(k + 1) * 128]))
        for k in range(8):
            dma("sync", ("xnts", k), [("xncp", NT - 1)],
                lambda e, k=k: e.dma_start(dxnt[k * 128:(k + 1) * 128, :], xnT[k][:, :]))

        NSC = S // 512
        for sc in range(NSC):
            pop_cvt(2)
            pbuf = P[0][:, 0:512] if sc % 2 == 0 else P[0][:, 512:1024]
            for e_ in range(8):
                op("tensor", None,
                   [("xncp", min(NT - 1, 4 * sc + 3)), ("c", "wqa"),
                    ("qk1cp", sc - 2) if sc >= 2 else None],
                   lambda e, e_=e_, sc=sc, pbuf=pbuf: e.matmul(
                       pbuf[0:64, :], wqa_b[:, e_ * 64:(e_ + 1) * 64],
                       xnT[e_][:, sc * 512:(sc + 1) * 512], start=(e_ == 0), stop=(e_ == 7)))
            for e_ in range(8):
                op("tensor", ("qk1", sc) if e_ == 7 else None, [("c", "wkva")],
                   lambda e, e_=e_, sc=sc, pbuf=pbuf: e.matmul(
                       pbuf[64:96, :], wkva_b[:, e_ * 32:(e_ + 1) * 32],
                       xnT[e_][:, sc * 512:(sc + 1) * 512], start=(e_ == 0), stop=(e_ == 7)))
            op("vector", None, [("qk1", sc)],
               lambda e, sc=sc, pbuf=pbuf: e.tensor_copy(qk1T[0:64, sc * 512:(sc + 1) * 512], pbuf[0:64, :]))
            op("vector", ("qk1cp", sc), [],
               lambda e, sc=sc, pbuf=pbuf: e.tensor_copy(qk1T[64:96, sc * 512:(sc + 1) * 512], pbuf[64:96, :]))

        def emit_kv_mm(it):
            b = it % 2
            pkv = P[1][:, 0:512] if b == 0 else P[1][:, 512:1024]
            op("tensor", None,
               [("qk1cp", it // 4), ("c", "wkvk"), ("vkv", it - 2) if it >= 2 else None],
               lambda e, it=it, pkv=pkv: e.matmul(
                   pkv[:, 0:256], qk1T[64:96, it * 128:(it + 1) * 128], wkvk_b[64:96, :], start=True, stop=True))
            op("tensor", ("kv", it), [("c", "wkvv")],
               lambda e, it=it, pkv=pkv: e.matmul(
                   pkv[:, 256:512], qk1T[64:96, it * 128:(it + 1) * 128], wkvv_b[64:96, :], start=True, stop=True))

        emit_kv_mm(0)
        for it in range(NT):
            b = it % 2
            pkv = P[1][:, 0:512] if b == 0 else P[1][:, 512:1024]
            if it + 1 < NT:
                emit_kv_mm(it + 1)
            pop_cvt(3)
            kv_guard = ("kv", it + 1) if it + 1 < NT else ("ktp", NT - 2)
            x1 = pkv[:, 0:256].rearrange("p (g u t) -> p g u t", u=2, t=32)[:, :, 0:1, :]
            x2 = pkv[:, 0:256].rearrange("p (g u t) -> p g u t", u=2, t=32)[:, :, 1:2, :]
            cosk = cosb[:, it * 32:(it + 1) * 32].rearrange("p (g o t) -> p g o t", g=1, o=1).broadcast_to([128, 4, 1, 32])
            sink = sinb[:, it * 32:(it + 1) * 32].rearrange("p (g o t) -> p g o t", g=1, o=1).broadcast_to([128, 4, 1, 32])
            t1 = rt1[:, 0:128].rearrange("p (g o t) -> p g o t", o=1, t=32)
            t2 = rt2[:, 0:128].rearrange("p (g o t) -> p g o t", o=1, t=32)
            kr1 = kroped[b][:, :].rearrange("p (g u t) -> p g u t", u=2, t=32)[:, :, 0:1, :]
            kr2 = kroped[b][:, :].rearrange("p (g u t) -> p g u t", u=2, t=32)[:, :, 1:2, :]
            op("vector", None, [kv_guard, ("c", "cos"), ("c", "sin"),
                                ("ktp", it - 2) if it >= 2 else None],
               lambda e, x1=x1, cosk=cosk, t1=t1: e.tensor_tensor(t1, x1, cosk, TT.mult))
            op("vector", None, [], lambda e, x2=x2, sink=sink, t2=t2: e.tensor_tensor(t2, x2, sink, TT.mult))
            op("vector", None, [], lambda e, kr1=kr1, t1=t1, t2=t2: e.tensor_tensor(kr1, t1, t2, TT.subtract))
            op("vector", None, [], lambda e, x1=x1, sink=sink, t1=t1: e.tensor_tensor(t1, x1, sink, TT.mult))
            op("vector", None, [], lambda e, x2=x2, cosk=cosk, t2=t2: e.tensor_tensor(t2, x2, cosk, TT.mult))
            op("vector", ("ropek", it), [], lambda e, kr2=kr2, t1=t1, t2=t2: e.tensor_tensor(kr2, t1, t2, TT.add))
            for g in range(4):
                op("vector", None, [("vpms",)],
                   lambda e, g=g, it=it, pkv=pkv: e.tensor_copy(
                       Vp[:, (g * NT + it) * 65:(g * NT + it) * 65 + 64], pkv[:, 256 + g * 64:256 + (g + 1) * 64]))
            op("vector", ("vkv", it), [], lambda e: e.tensor_copy(c1[:, :], c2[:, :]))
            for half in range(2):
                op("tensor", ("ktp", it) if half == 1 else None,
                   [("ropek", it), ("ktcp", it - 1) if it >= 1 else None],
                   lambda e, half=half, b=b: e.transpose(
                       QX2[:, half * 128:(half + 1) * 128], kroped[b][:, half * 128:(half + 1) * 128], identb[:, :]))
            for half in range(2):
                kdst = kT0 if half == 0 else kT1
                op("scalar", ("ktcp", it) if half == 1 else None, [("ktp", it)],
                   lambda e, half=half, it=it, kdst=kdst: e.copy(
                       kdst[:, it * 128:(it + 1) * 128], QX2[:, half * 128:(half + 1) * 128]))
        while cvt_pos["p"] < len(cvt_units):
            pop_cvt(1)


        barrier_evs = ([("wdt", 3, NHD - 1), ("ktp", NT - 1), ("qk1", NSC - 1),
                        ("vkv", NT - 1), ("ropek", NT - 1), ("xn", NT - 1),
                        ("wdc", 3, NHD - 1), ("xncp", NT - 1), ("ktcp", NT - 1),
                        ("wdl", 3, 1)]
                       + [("wds", 3, Hh) for Hh in range(NHD - 8, NHD)]
                       + [("xnts", k) for k in range(8)])
        for engname in ("sync", "gpsimd", "tensor", "vector", "scalar"):
            own = SEM_OF[engname]
            ctx.op(engname, ("mb", engname),
                   [ev for ev in barrier_evs if ctx.events[ev][0] != own],
                   lambda e: e.nop())

        NLD = NHD // 2

        def emit_xns(ic):
            for k in range(8):
                dma("sync", ("xnl", ic) if k == 7 else None,
                    [("xnts", 7), ("gu", ic - 1, NHD - 1) if ic >= 1 else None],
                    lambda e, k=k, ic=ic: e.dma_start(
                        xnS[0][:, k * 512:(k + 1) * 512],
                        dxnt[k * 128:(k + 1) * 128, ic * 512:(ic + 1) * 512]))

        def emit_F1_dma(ic1, hd):
            b = hd % 2
            wl_w = [("ws", "wg", NHD - 1)]
            if hd >= 2:
                wl_w.append(("gu", ic1, hd - 2))
            elif ic1 >= 1:
                wl_w.append(("gu", ic1 - 1, NHD - 2 + hd))
            dma("gpsimd", ("wgl", ic1, hd), wl_w,
                lambda e, hd=hd, b=b: e.dma_start(wgB[b][:, :], dwgt[hd * 128:(hd + 1) * 128, :]))
            dma("gpsimd", ("wul", ic1, hd), [("ws", "wu", NHD - 1)],
                lambda e, hd=hd, b=b: e.dma_start(wuB[b][:, :], dwut[hd * 128:(hd + 1) * 128, :]))

        def emit_F1_unit(ic1, hd, part):
            b = hd % 2
            psG = P[1][:, 0:512]
            psU = P[1][:, 512:1024]
            if part == 0:
                gu_w = [("wgl", ic1, hd), ("wul", ic1, hd), ("xnl", ic1)]
                if hd >= 1:
                    gu_w.append(("gb2", ic1, hd - 1))
                elif ic1 >= 1:
                    gu_w.append(("rope", ic1 - 1, TPC - 1))
                for e_ in range(8):
                    op("tensor", None, gu_w if e_ == 0 else [],
                       lambda e, e_=e_, psG=psG: e.matmul(
                           psG, wgB[b][:, e_ * 128:(e_ + 1) * 128], xnS[0][:, e_ * 512:(e_ + 1) * 512],
                           start=(e_ == 0), stop=(e_ == 7)))
                return
            for e_ in range(8):
                op("tensor", ("gu", ic1, hd) if e_ == 7 else None,
                   ([("h", ic1, hd - 1)] if hd >= 1 else []) if e_ == 0 else [],
                   lambda e, e_=e_, psU=psU: e.matmul(
                       psU, wuB[b][:, e_ * 128:(e_ + 1) * 128], xnS[0][:, e_ * 512:(e_ + 1) * 512],
                       start=(e_ == 0), stop=(e_ == 7)))
            sg_w = [("gu", ic1, hd)]
            if hd >= 2:
                sg_w.append(("h", ic1, hd - 2))
            op("scalar", ("sg", ic1, hd), sg_w,
               lambda e, psG=psG, b=b: e.activation(eF[b][:, :], psG, AF.Tanh, scale=DEQ * 0.5))
            h_w = [("sg", ic1, hd)]
            if hd == 0 and ic1 >= 1:
                h_w.append(("dn", ic1 - 1, 1, 1, NLD - 1))
            op("vector", None, h_w,
               lambda e, b=b: e.tensor_scalar_add(eF[b][:, :], eF[b][:, :], 1.0))
            op("vector", None, [], lambda e: e.tensor_copy(jnk[:, 0:1], jnk[:, 1:2]))
            op("vector", ("gb2", ic1, hd), [],
               lambda e, psG=psG, b=b: e.scalar_tensor_tensor(
                   gB2[:, :], psG, DEQ * DEQ * 0.5, eF[b][:, :], TT.mult, TT.mult))
            op("vector", None, [], lambda e: e.tensor_copy(jnk[:, 0:1], jnk[:, 1:2]))
            op("vector", ("h", ic1, hd), [],
               lambda e, hd=hd, psU=psU: e.tensor_mul(
                   hT[:, hd * 512:(hd + 1) * 512], gB2[:, :], psU))

        wdBv = [wdB[0][:, 0:1024], wdB[0][:, 1024:2048],
                wdB[1][:, 0:1024], wdB[1][:, 1024:2048]]

        def emit_F2_dma(ic1, h2, seg, ld):
            b = ld % 4
            wdl_w = [("wds", 3, NHD - 1)]
            q4 = 2 * h2 + seg
            if ld >= 4:
                wdl_w.append(("dn", ic1, h2, seg, ld - 4))
            elif q4 >= 1:
                wdl_w.append(("dn", ic1, (q4 - 1) // 2, (q4 - 1) % 2, NLD - 4 + ld))
            elif ic1 >= 1:
                wdl_w.append(("dn", ic1 - 1, 1, 1, NLD - 4 + ld))
            dma("sync", ("wdl2", ic1, h2, seg, ld), wdl_w,
                lambda e, ld=ld, b=b, seg=seg: e.dma_start(
                    wdBv[b].rearrange("p (c k) -> p c k", k=512),
                    dwdt[ld * 256:(ld + 1) * 256, seg * 512:(seg + 1) * 512]
                    .rearrange("(c p) k -> p c k", p=128)))

        def emit_F2_unit(ic1, h2, seg, ld):
            b = ld % 4
            q4 = 2 * h2 + seg
            psD2 = [P[1][:, 0:512], P[1][:, 512:1024]]
            dn_w = [("wdl2", ic1, h2, seg, ld)]
            if ld == 0:
                if q4 == 0:
                    dn_w += [("h", ic1, NHD - 1)]
                else:
                    dn_w += [("gs", ic1, (q4 - 1) // 2, (q4 - 1) % 2)]
            for hd2 in range(2):
                hd = 2 * ld + hd2
                for tl in range(2):
                    t = 2 * h2 + tl
                    op("tensor",
                       ("dn", ic1, h2, seg, ld) if (hd2 == 1 and tl == 1) else None,
                       dn_w if (hd2 == 0 and tl == 0) else [],
                       lambda e, hd2=hd2, seg=seg, hd=hd, b=b, t=t, tl=tl: e.matmul(
                           psD2[tl],
                           hT[:, hd * 512 + t * 128:hd * 512 + (t + 1) * 128],
                           wdBv[b][:, hd2 * 512:(hd2 + 1) * 512],
                           start=(ld == 0 and hd2 == 0), stop=(ld == NLD - 1 and hd2 == 1)))
            if ld == NLD - 1:
                gb = (ic1 % 2) * TPC
                gs_w = [("dn", ic1, h2, seg, NLD - 1)]
                for tl in range(2):
                    t = 2 * h2 + tl
                    op("scalar", ("ge", ic1, t, seg) if tl == 1 else None,
                       gs_w if tl == 0 else [],
                       lambda e, tl=tl: e.activation(
                           eG[tl][:, 0:512], psD2[tl], AF.Tanh, scale=DEQ * 0.5))
                gv_w = [("ge", ic1, 2 * h2 + 1, seg)]
                if ic1 >= 2:
                    gv_w.append(("comb", ic1 - 2, TPC - 1))
                for tl in range(2):
                    op("vector", None, gv_w if tl == 0 else [],
                       lambda e, tl=tl: e.tensor_scalar_add(eG[tl][:, 0:512], eG[tl][:, 0:512], 1.0))
                for tl in range(2):
                    t = 2 * h2 + tl
                    op("vector", ("gs", ic1, h2, seg) if tl == 1 else None, [],
                       lambda e, t=t, tl=tl, gb=gb, seg=seg: e.scalar_tensor_tensor(
                           g_nat[gb + t][:, seg * 512:(seg + 1) * 512], psD2[tl],
                           DEQ * 0.5, eG[tl][:, 0:512], TT.mult, TT.mult))

        def emit_W_tile(ic, t):
            it = ic * TPC + t
            gb = (ic % 2) * TPC
            xr_w = [("ost", ic, t - 2) if t >= 2 else (("ost", ic - 1, t + 2) if ic > 0 else None),
                    ("comb", ic, t - 2) if t >= 2 else (("comb", ic - 1, t + 2) if ic > 0 else None)]
            dma("gpsimd", ("xr", ic, t), xr_w,
                lambda e, it=it, t=t: e.dma_start(outb[t % 2][:, :], dx[it * 128:(it + 1) * 128, :]))
            wo_w = [("c", "wout"), ("c", "bout"), ("ots", ic, 0)]
            if t == 0:
                wo_w.append(("exp", ic, 7, TPC * ic + TPC - 1))
            else:
                wo_w.append(("comb", ic, t - 1))
            for pr in range(8):
                pr_w = wo_w if (pr == 0) else [("ots", ic, pr)]
                for seg in range(2):
                    op("tensor", None, pr_w if seg == 0 else [],
                       lambda e, pr=pr, seg=seg, t=t: e.matmul(
                           P[0][:, seg * 512:(seg + 1) * 512], ot[pr][:, t * 128:(t + 1) * 128],
                           wout_pair[:, pr * E + seg * 512:pr * E + (seg + 1) * 512],
                           start=(pr == 0), stop=False))
            for seg in range(2):
                op("tensor", ("wo", ic, t) if seg == 1 else None, [],
                   lambda e, seg=seg: e.matmul(
                       P[0][:, seg * 512:(seg + 1) * 512], onesb[0:1, :],
                       bout_b[0:1, seg * 512:(seg + 1) * 512], start=False, stop=True))
            comb_w = [("wo", ic, t), ("xr", ic, t),
                      ("gs", ic, t // 2, 0), ("gs", ic, t // 2, 1)]
            if t == 0:
                comb_w.append(("rrep", ic, 7))
            op("vector", None, comb_w,
               lambda e, t=t, gb=gb: e.tensor_mul(sgtmp[:, :], P[0][:, :], g_nat[gb + t][:, :]))
            op("vector", ("comb", ic, t), [],
               lambda e, t=t: e.tensor_add(outb[t % 2][:, :], sgtmp[:, :], outb[t % 2][:, :]))
            dma("gpsimd", ("ost", ic, t), [("comb", ic, t)],
                lambda e, it=it, t=t: e.dma_start(dout[it * 128:(it + 1) * 128, :], outb[t % 2][:, :]),
                sem="OS")

        def emit_Q_tile(ic, t):
            it = ic * TPC + t
            mm_waits = [("qk1cp", NSC - 1), ("c", "wqb")]
            if t == 0:
                mm_waits.append(("gs", ic, 1, 1))
            else:
                mm_waits.append(("rope", ic, t - 1))
            for seg in range(2):
                op("tensor", ("mmq", ic, t) if seg == 1 else None, mm_waits if seg == 0 else [],
                   lambda e, it=it, seg=seg: e.matmul(
                       P[1][:, seg * 512:(seg + 1) * 512], qk1T[0:64, it * 128:(it + 1) * 128],
                       wqb_b[:, seg * 512:(seg + 1) * 512], start=True, stop=True))
            if ic == 0:
                op("tensor", ("qdg", ic, t), [],
                   lambda e: e.matmul(P[2][64:128, 0:64], onesb[0:1, 0:64],
                                      bout_b[0:1, 0:64], start=True, stop=True))
            pq = P[1]
            x1 = pq[:, :].rearrange("p (h u t) -> p h u t", u=2, t=32)[:, :, 0:1, :]
            x2 = pq[:, :].rearrange("p (h u t) -> p h u t", u=2, t=32)[:, :, 1:2, :]
            cosq = cosb[:, it * 32:(it + 1) * 32].rearrange("p (g o t) -> p g o t", g=1, o=1).broadcast_to([128, 16, 1, 32])
            sinq = sinb[:, it * 32:(it + 1) * 32].rearrange("p (g o t) -> p g o t", g=1, o=1).broadcast_to([128, 16, 1, 32])
            t1 = rt1[:, :].rearrange("p (h o t) -> p h o t", o=1, t=32)
            t2 = rt2[:, :].rearrange("p (h o t) -> p h o t", o=1, t=32)
            qb = t % 2
            qr1 = qroped[qb][:, :].rearrange("p (h u t) -> p h u t", u=2, t=32)[:, :, 0:1, :]
            qr2 = qroped[qb][:, :].rearrange("p (h u t) -> p h u t", u=2, t=32)[:, :, 1:2, :]
            rope_w = [("qdg", ic, t) if ic == 0 else ("wo", ic - 1, t)]
            if t >= 2:
                rope_w.append(("qxp", ic, t - 2))
            elif ic > 0:
                rope_w.append(("qxp", ic - 1, t + 2))
            op("vector", None, rope_w, lambda e, x1=x1, cosq=cosq, t1=t1: e.tensor_tensor(t1, x1, cosq, TT.mult))
            op("vector", None, [], lambda e, x2=x2, sinq=sinq, t2=t2: e.tensor_tensor(t2, x2, sinq, TT.mult))
            op("vector", None, [], lambda e, qr1=qr1, t1=t1, t2=t2: e.tensor_tensor(qr1, t1, t2, TT.subtract))
            op("vector", None, [], lambda e, x1=x1, sinq=sinq, t1=t1: e.tensor_tensor(t1, x1, sinq, TT.mult))
            op("vector", None, [], lambda e, x2=x2, cosq=cosq, t2=t2: e.tensor_tensor(t2, x2, cosq, TT.mult))
            op("vector", ("rope", ic, t), [], lambda e, qr2=qr2, t1=t1, t2=t2: e.tensor_tensor(qr2, t1, t2, TT.add))

        def emit_qxp(ic, t):
            qb = t % 2
            psx = QX if t % 2 == 0 else QX2
            xp_w = [("rope", ic, t)]
            if t >= 2:
                xp_w.append(("qtcp", ic, t - 2))
            elif ic > 0:
                xp_w.append(("qtcp", ic - 1, TPC - 1))
            if t == 0 and ic > 0:
                xp_w.append(("exp", ic - 1, 7, TPC * ic - 1))
            for k in range(8):
                op("tensor", ("qxp", ic, t) if k == 7 else None, xp_w if k == 0 else [],
                   lambda e, k=k, qb=qb, psx=psx: e.transpose(
                       psx[:, k * 128:(k + 1) * 128], qroped[qb][:, k * 128:(k + 1) * 128], identb[:, :]))
            qtcp_w = [("qxp", ic, t)]
            if ic > 0:
                qtcp_w.append(("qk", ic - 1, 7, TPC * ic - 1))
            for k in range(8):
                op("scalar", ("qtcp", ic, t) if k == 7 else None, qtcp_w if k == 0 else [],
                   lambda e, k=k, t=t, psx=psx: e.copy(
                       qTc[k][:, t * 128:(t + 1) * 128], psx[:, k * 128:(k + 1) * 128]))

        def emit_A_step(ic, p, jt, grs, jtlast):
            b = jt % 4
            psS = P3 if jt % 2 == 0 else P[0]
            for seg in range(2):
                qk_w = []
                if seg == 0:
                    qk_w = [("qtcp", ic, TPC - 1), ("ktcp", NT - 1), ("bias", ic)]
                    if jt >= 2:
                        qk_w.append(("exp", ic, p, jt - 2))
                    elif p > 0:
                        qk_w.append(("exp", ic, p - 1, jtlast - 1 + jt))
                    elif ic > 0:
                        qk_w.append(("exp", ic - 1, 7, jtlast - TPC - 1 + jt))
                    if jt == 1 and p == 0 and ic > 0:
                        qk_w.append(("comb", ic - 1, TPC - 1))
                ktsrc = kT0 if grs[seg] < 2 else kT1
                op("tensor", ("qk", ic, p, jt) if seg == 1 else None, qk_w,
                   lambda e, p=p, jt=jt, seg=seg, ktsrc=ktsrc, psS=psS: e.matmul(
                       psS[:, seg * 512:(seg + 1) * 512],
                       ktsrc[seg * 64:seg * 64 + 64, jt * 128:(jt + 1) * 128],
                       qTc[p][seg * 64:(seg + 1) * 64, :], start=True, stop=True))
            m = jt - TPC * ic
            exp_w = [("qk", ic, p, jt)]
            if jt >= 4:
                exp_w.append(("pv", ic, p, jt - 4))
            elif p > 0:
                exp_w.append(("pv", ic, p - 1, jtlast - 3 + jt))
            elif ic > 0:
                exp_w.append(("pv", ic - 1, 7, jtlast - TPC - 3 + jt))
            op("scalar", ("exp", ic, p, jt), exp_w,
               lambda e, b=b, psS=psS: e.activation(Pt[b][:, :], psS[:, :], AF.Exp, scale=SCALE))
            if m >= 0:
                for seg in range(2):
                    op("vector", ("mask", ic, p, jt, seg),
                       [("exp", ic, p, jt), ("c", "masks")],
                       lambda e, seg=seg, b=b, m=m: e.tensor_mul(
                           Pt[b][:, seg * 512:(seg + 1) * 512], Pt[b][:, seg * 512:(seg + 1) * 512],
                           masksb[:, m * 512:(m + 1) * 512]))

        def emit_pv(ic, p, jt, grs, jtlast):
            b = jt % 4
            m = jt - TPC * ic
            for seg in range(2):
                dep = ("mask", ic, p, jt, seg) if m >= 0 else ("exp", ic, p, jt)
                pv_w = [dep]
                if seg == 0:
                    pv_w.append(("vkv", NT - 1))
                    if jt == 0 and p > 0:
                        pv_w += [("ocp", ic, p - 1), ("lcp", ic, p - 1)]
                lbl = None
                if seg == 1:
                    lbl = ("pv", ic, p, jt)
                elif jt == jtlast:
                    lbl = ("pv0", ic, p)
                op("tensor", lbl, pv_w,
                   lambda e, seg=seg, b=b, gr=grs[seg], jt=jt: e.matmul(
                       P[2][0:65, seg * 512:(seg + 1) * 512],
                       Vp[:, (gr * NT + jt) * 65:(gr * NT + jt + 1) * 65],
                       Pt[b][:, seg * 512:(seg + 1) * 512],
                       start=(jt == 0), stop=(jt == jtlast)))

        def emit_A_tailA(ic, p, grs, jtlast):
            for jt in range(max(0, jtlast - 3), jtlast + 1):
                emit_pv(ic, p, jt, grs, jtlast)
            for q in range(4):
                op("scalar", ("lcp0", ic, p) if q == 1 else
                   (("lcp", ic, p) if q == 3 else None),
                   [("pv0", ic, p),
                    ("comb", ic - 1, TPC - 1) if (p == 0 and ic > 0) else None,
                    ("rrep", ic, p - 1) if p > 0 else
                    (("rrep", ic - 1, 7) if ic > 0 else None)] if q == 0 else
                   ([("pv", ic, p, jtlast)] if q == 2 else []),
                   lambda e, q=q: e.copy(recl[q * 32:q * 32 + 1, :],
                                         P[2][64:65, q * 256:(q + 1) * 256]))
            op("scalar", None, [("pv0", ic, p)],
               lambda e: e.copy(Of32[:, 0:512], P[2][0:64, 0:512]))
            op("scalar", ("ocp", ic, p), [("pv", ic, p, jtlast)],
               lambda e: e.copy(Of32[:, 512:1024], P[2][0:64, 512:1024]))

        def emit_A_tailB(ic, p, grs, jtlast):
            op("vector", ("rec", ic, p), [("lcp0", ic, p), ("lcp", ic, p)],
               lambda e: e.reciprocal(recl[:, :], recl[:, :]))
            for q in range(4):
                dma("gpsimd", ("rst", ic, p, q), [("rec", ic, p)] if q == 0 else [],
                    lambda e, q=q: e.dma_start(
                        drecl[0:1, q * 256:(q + 1) * 256], recl[q * 32:q * 32 + 1, :]))
            dma("gpsimd", ("rrep", ic, p),
                [("rst", ic, p, 0), ("rst", ic, p, 1), ("rst", ic, p, 2),
                 ("rst", ic, p, 3), ("ot", ic, p - 1) if p > 0 else
                 (("ot", ic - 1, 7) if ic > 0 else None)],
                lambda e: e.dma_start(rec_rep[:, :],
                                      drecl.broadcast_to([64, 1024])))
            op("vector", None, [("rrep", ic, p), ("ots", ic, p - 1) if p >= 1 else
                                (("ots", ic - 1, 7) if ic > 0 else None)],
               lambda e, p=p: e.tensor_mul(ot[p][0:64, :], Of32[:, 0:512], rec_rep[:, 0:512]))
            op("vector", ("ot", ic, p), [],
               lambda e, p=p: e.tensor_mul(ot_stg[:, :], Of32[:, 512:1024], rec_rep[:, 512:1024]))
            dma("gpsimd", ("ots", ic, p), [("ot", ic, p)],
                lambda e, p=p: e.dma_start(ot[p][64:128, :], ot_stg[:, :]))

        emit_xns(0)
        for hd in range(NHD):
            emit_F1_dma(0, hd)
            emit_F1_unit(0, hd, 0)
            emit_F1_unit(0, hd, 1)
        for h2 in range(2):
            for seg in range(2):
                for ld in range(NLD):
                    emit_F2_dma(0, h2, seg, ld)
                    emit_F2_unit(0, h2, seg, ld)

        for ic in range(NIC):
            jtlast = TPC * ic + TPC - 1
            if ic + 1 < NIC:
                emit_xns(ic + 1)
            for t in range(TPC):
                emit_Q_tile(ic, t)
                if ic > 0:
                    emit_W_tile(ic - 1, t)
                if t >= 1:
                    emit_qxp(ic, t - 1)
            emit_qxp(ic, TPC - 1)
            op("scalar", ("bias", ic), [], lambda e: e.copy(c3[:, :], c4[:, :]))

            f_units = []
            if ic + 1 < NIC:
                for hd in range(NHD):
                    f_units.append((lambda hd=hd, ic1=ic + 1: emit_F1_dma(ic1, hd),
                                    lambda hd=hd, ic1=ic + 1: emit_F1_unit(ic1, hd, 0)))
                    f_units.append((None,
                                    lambda hd=hd, ic1=ic + 1: emit_F1_unit(ic1, hd, 1)))
                for h2 in range(2):
                    for seg in range(2):
                        for ld in range(NLD):
                            f_units.append((lambda h2=h2, seg=seg, ld=ld, ic1=ic + 1: emit_F2_dma(ic1, h2, seg, ld),
                                            lambda h2=h2, seg=seg, ld=ld, ic1=ic + 1: emit_F2_unit(ic1, h2, seg, ld)))
            grs_of = [[HEAD_OF_POS[2 * p] // 4, HEAD_OF_POS[2 * p + 1] // 4] for p in range(8)]
            steps = [(p, jt) for p in range(8) for jt in range(jtlast + 1)]
            nf = len(f_units)
            ns = len(steps)
            ns_eff = max(1, ns - max(2, ns // 8))
            fpos = {"d": 0, "p": 0}

            def pop_f(target):
                while fpos["p"] < target and fpos["p"] < nf:
                    while fpos["d"] < min(fpos["p"] + 6, nf):
                        d = f_units[fpos["d"]][0]
                        if d is not None:
                            d()
                        fpos["d"] += 1
                    f_units[fpos["p"]][1]()
                    fpos["p"] += 1

            pend_tailB = None
            for si, (p, jt) in enumerate(steps):
                if jt >= 4:
                    emit_pv(ic, p, jt - 4, grs_of[p], jtlast)
                if jt == 2 and pend_tailB is not None:
                    emit_A_tailB(ic, pend_tailB, grs_of[pend_tailB], jtlast)
                    pend_tailB = None
                pop_f(nf * min(si + 1, ns_eff) // ns_eff)
                emit_A_step(ic, p, jt, grs_of[p], jtlast)
                if jt == jtlast:
                    emit_A_tailA(ic, p, grs_of[p], jtlast)
                    pend_tailB = p
            pop_f(nf)
            emit_A_tailB(ic, pend_tailB, grs_of[pend_tailB], jtlast)

        for t in range(TPC):
            emit_W_tile(NIC - 1, t)

        ctx.wait_only("gpsimd", [("ost", NIC - 1, TPC - 1)])

        if DEBUG_DUMPS:
            dwaits = [("ost", NIC - 1, TPC - 1), ("comb", NIC - 1, TPC - 1),
                      ("ot", NIC - 1, 7), ("gs", NIC - 1, 1, 1),
                      ("exp", NIC - 1, 7, TPC * NIC - 1), ("bias", NIC - 1)]
            for nm, sb in dump_list:
                dma("sync", ("dump", nm), dwaits,
                    lambda e, nm=nm, sb=sb: e.dma_start(dump_drams[nm], sb))

    events = {}
    rctx = Ctx(record=True, events=events)
    program(rctx)
    build_nc.last_counts = dict(rctx.counts)
    build_nc.last_rctx = rctx

    ksr = bass.get_kernel_semaphore_range()
    n_keep = 2 + (1 if nc._bir_kernel_barrier_sem is not None else 0) + len(nc._monotonic_sems)
    reset_range = range(ksr.start + n_keep, ksr.stop)
    with nc.Block() as blk0:
        @blk0.gpsimd
        def _(g):
            g.dma_reset(reset_range)
            g.sem_clear(reset_range)

    with nc.Block() as blk:
        @blk.sync
        def _(eng):
            program(Ctx(record=False, engine="sync", eng=eng, events=events, sems=sems))

        @blk.gpsimd
        def _(eng):
            program(Ctx(record=False, engine="gpsimd", eng=eng, events=events, sems=sems))

        @blk.tensor
        def _(eng):
            program(Ctx(record=False, engine="tensor", eng=eng, events=events, sems=sems))

        @blk.vector
        def _(eng):
            program(Ctx(record=False, engine="vector", eng=eng, events=events, sems=sems))

        @blk.scalar
        def _(eng):
            program(Ctx(record=False, engine="scalar", eng=eng, events=events, sems=sems))

    for g in reversed(sem_guards):
        g.__exit__(None, None, None)
    return nc


def make_in_map(cfg, x_core, Wqa, Wqb, Wkva, Wkvb, Wout, bout, wg, wu, wd):
    cos, sin, ident, ones, masks, perm_q, perm_k, perm_v = host_constants(cfg.S)
    return {
        "x": np.ascontiguousarray(x_core),
        "wqa": np.ascontiguousarray(Wqa),
        "wqb": np.ascontiguousarray(Wqb[:, perm_q]),
        "wkva": np.ascontiguousarray(Wkva),
        "wkvk": np.ascontiguousarray(Wkvb[:, perm_k]),
        "wkvv": np.ascontiguousarray(Wkvb[:, perm_v]),
        "woutp": np.ascontiguousarray(np.concatenate(
            [Wout[HEAD_OF_POS[q] * 64:(HEAD_OF_POS[q] + 1) * 64]
             for q in range(16)], axis=0)),
        "bout": np.ascontiguousarray(bout.reshape(1, E)),
        "wg": np.ascontiguousarray(wg),
        "wu": np.ascontiguousarray(wu),
        "wd": np.ascontiguousarray(wd),
        "cost": cos, "sint": sin, "ident": ident, "ones": ones, "masks": masks,
    }


def kernel(L0_attn_x, L0_attn_Wqa, L0_attn_Wqb, L0_attn_Wkva, L0_attn_Wkvb,
           L0_attn_Wout, L0_attn_bout, L0_ffn_w_gate, L0_ffn_w_up, L0_ffn_w_down):
    cfg = Cfg(S=4096, HID=3072)
    nc = build_nc(cfg)
    x = np.asarray(L0_attn_x)
    in_maps = [
        make_in_map(cfg, x[c], np.asarray(L0_attn_Wqa), np.asarray(L0_attn_Wqb),
                    np.asarray(L0_attn_Wkva), np.asarray(L0_attn_Wkvb),
                    np.asarray(L0_attn_Wout), np.asarray(L0_attn_bout),
                    np.asarray(L0_ffn_w_gate), np.asarray(L0_ffn_w_up),
                    np.asarray(L0_ffn_w_down))
        for c in range(8)
    ]
    res = run_bass_kernel_spmd(nc, in_maps, core_ids=list(range(8)))
    return np.stack([np.asarray(r["out"], dtype=np.float32) for r in res.results])


# version 19 (final)
# speedup vs baseline: 1.0389x; 1.0389x over previous
import contextlib
import numpy as np
import ml_dtypes
import concourse.bass as bass
import concourse.mybir as mybir
from concourse.bass_utils import run_bass_kernel_spmd

F32 = mybir.dt.float32
BF16 = mybir.dt.bfloat16
I32 = mybir.dt.int32
AF = mybir.ActivationFunctionType
ALU = mybir.AluOpType
AX = mybir.AxisListType

E = 1024
H_HEADS = 16
KVH = 4
D = 64
IC = 512
SCALE = 0.125
DEQ = float(np.float32(1.0 / 127.0))

SEM_OF = {"sync": "SY", "gpsimd": "GP", "tensor": "T", "vector": "V", "scalar": "SC"}
DMA_QUEUE_OF = {"sync": "DS", "gpsimd": "DG"}

HEAD_OF_POS = [0, 4, 1, 5, 2, 6, 3, 7, 8, 12, 9, 13, 10, 14, 11, 15]

DEBUG_DUMPS = False
DEBUG_LN = False


class Cfg:
    def __init__(self, S=4096, HID=3072):
        self.S = S
        self.HID = HID
        self.NT = S // 128
        self.NIC = S // IC
        self.TPC = IC // 128
        self.NHD = HID // 128


DMA_LANES = {"DS": 8, "DG": 4, "OS": 2}


class Ctx:
    def __init__(self, record, engine=None, eng=None, events=None, sems=None):
        self.record = record
        self.engine = engine
        self.eng = eng
        self.events = events if events is not None else {}
        self.sems = sems
        self.counts = {}
        self.qpos = {q: 0 for q in DMA_LANES}
        self.hwm = {}
        self.trace_log = []

    def _bump(self, sem_name, inc):
        self.counts[sem_name] = self.counts.get(sem_name, 0) + inc
        return self.counts[sem_name]

    def _wait(self, sem_name, v):
        if self.hwm.get(sem_name, 0) < v:
            self.eng.wait_ge(self.sems[sem_name], v)
            self.hwm[sem_name] = v

    def op(self, engine, label, waits, emit, inc=1):
        sem_name = SEM_OF[engine]
        if self.record:
            if label is not None:
                assert label not in self.events, f"dup label {label}"
                self.events[label] = (sem_name, self._bump(sem_name, inc))
            self.trace_log.append((engine, list(waits), label,
                                   sem_name if label is not None else None,
                                   inc, None))
            return
        if engine != self.engine:
            return
        for w in waits:
            if w is None:
                continue
            s, v = self.events[w]
            self._wait(s, v)
        ins = emit(self.eng)
        if label is not None:
            ins.then_inc(self.sems[sem_name], inc)

    def dma(self, engine, label, waits, emit, sem=None):
        q = sem or DMA_QUEUE_OF[engine]
        lane = self.qpos[q] % DMA_LANES[q]
        self.qpos[q] += 1
        sem_name = f"{q}{lane}"
        if self.record:
            v = self._bump(sem_name, 16)
            if label is not None:
                assert label not in self.events, f"dup label {label}"
                self.events[label] = (sem_name, v)
            self.trace_log.append((engine, list(waits), label, sem_name, 16,
                                   (sem_name, v - 16) if v > 16 else None))
            return
        prev = self.counts.get(sem_name, 0)
        self.counts[sem_name] = prev + 16
        if engine != self.engine:
            return
        for w in waits:
            if w is None:
                continue
            s, v = self.events[w]
            self._wait(s, v)
        if prev > 0:
            self._wait(sem_name, prev)
        ins = emit(self.eng)
        ins.then_inc(self.sems[sem_name], 16)

    def wait_only(self, engine, waits):
        self.op(engine, None, waits, lambda e: e.nop())


ALLOC_LOG = []


class SBAlloc:
    def __init__(self, nc, base=16512, top=229344):
        self.nc = nc
        self.cur = base
        self.top = top

    def alloc(self, name, shape, dtype):
        sz = {F32: 4, BF16: 2, I32: 4}[dtype]
        nb = sz
        for d_ in shape[1:]:
            nb *= d_
        off = (self.cur + 31) // 32 * 32
        self.cur = off + nb
        assert self.cur <= self.top, f"SBUF overflow at {name}: {self.cur}"
        ALLOC_LOG.append((name, off, self.cur, shape[0]))
        return self.nc.alloc_sbuf_tensor_at(name, list(shape), dtype, offset=off)

    def mark(self):
        return self.cur

    def reset(self, m):
        self.cur = m


def host_constants(S):
    inv = 1.0 / (10000.0 ** (np.arange(0, D, 2, dtype=np.float32) / D))
    t = np.arange(S, dtype=np.float32)
    fr = np.outer(t, inv)
    cos = np.cos(fr).astype(np.float32)
    sin = np.sin(fr).astype(np.float32)
    ident = np.eye(128, dtype=np.float32)
    ones = np.ones((1, 128), dtype=np.float32)
    masks = np.zeros((128, 2048), dtype=np.float32)
    jj = np.arange(128)[:, None]
    ii = np.arange(512)[None, :]
    for m in range(4):
        masks[:, m * 512:(m + 1) * 512] = (ii >= 128 * m + jj)
    perm_q = np.zeros(1024, np.int64)
    for pos in range(16):
        h = HEAD_OF_POS[pos]
        for tt in range(32):
            perm_q[pos * 64 + tt] = h * 64 + 2 * tt
            perm_q[pos * 64 + 32 + tt] = h * 64 + 2 * tt + 1
    perm_k = np.zeros(256, np.int64)
    perm_v = np.zeros(256, np.int64)
    for g in range(4):
        for tt in range(32):
            perm_k[g * 64 + tt] = g * 128 + 4 * tt
            perm_k[g * 64 + 32 + tt] = g * 128 + 4 * tt + 2
        for d_ in range(64):
            perm_v[g * 64 + d_] = g * 128 + 2 * d_ + 1
    return cos, sin, ident, ones, masks, perm_q, perm_k, perm_v


def build_nc(cfg):
    S, HID, NT, NIC, TPC, NHD = cfg.S, cfg.HID, cfg.NT, cfg.NIC, cfg.TPC, cfg.NHD
    nc = bass.Bass("TRN2", target_bir_lowering=False)

    dx = nc.dram_tensor("x", [S, E], F32, kind="ExternalInput").ap()
    dwqa = nc.dram_tensor("wqa", [E, 64], F32, kind="ExternalInput").ap()
    dwqb = nc.dram_tensor("wqb", [64, E], F32, kind="ExternalInput").ap()
    dwkva = nc.dram_tensor("wkva", [E, 32], F32, kind="ExternalInput").ap()
    dwkvk = nc.dram_tensor("wkvk", [32, 256], F32, kind="ExternalInput").ap()
    dwkvv = nc.dram_tensor("wkvv", [32, 256], F32, kind="ExternalInput").ap()
    dwoutp = nc.dram_tensor("woutp", [E, E], F32, kind="ExternalInput").ap()
    dbout = nc.dram_tensor("bout", [1, E], F32, kind="ExternalInput").ap()
    dwg = nc.dram_tensor("wg", [HID, E], I32, kind="ExternalInput").ap()
    dwu = nc.dram_tensor("wu", [HID, E], I32, kind="ExternalInput").ap()
    dwd = nc.dram_tensor("wd", [E, HID], I32, kind="ExternalInput").ap()
    dcos = nc.dram_tensor("cost", [S, 32], F32, kind="ExternalInput").ap()
    dsin = nc.dram_tensor("sint", [S, 32], F32, kind="ExternalInput").ap()
    dident = nc.dram_tensor("ident", [128, 128], F32, kind="ExternalInput").ap()
    dones = nc.dram_tensor("ones", [1, 128], F32, kind="ExternalInput").ap()
    dmasks = nc.dram_tensor("masks", [128, 2048], F32, kind="ExternalInput").ap()
    dout = nc.dram_tensor("out", [S, E], F32, kind="ExternalOutput").ap()
    dxnt = nc.dram_tensor("xnt_s", [E, S], BF16, kind="Internal").ap()
    drecl = nc.dram_tensor("recl_s", [1, 1024], F32, kind="Internal").ap()
    dwgt = nc.dram_tensor("wgt_s", [NHD * 128, E], BF16, kind="Internal").ap()
    dwut = nc.dram_tensor("wut_s", [NHD * 128, E], BF16, kind="Internal").ap()
    dwdt = nc.dram_tensor("wdt_s", [NHD * 128, E], BF16, kind="Internal").ap()

    A = SBAlloc(nc)
    kT0 = A.alloc("kT0", [128, S], BF16)
    kT1 = A.alloc("kT1", [128, S], BF16)
    Vp = A.alloc("Vp", [128, KVH * NT * 65], BF16)
    qk1T = A.alloc("qk1T", [128, S], BF16)
    cosb = A.alloc("cosb", [128, NT * 32], F32)
    sinb = A.alloc("sinb", [128, NT * 32], F32)
    identb = A.alloc("identb", [128, 128], BF16)
    onesb = A.alloc("onesb", [1, 128], BF16)
    masksb = A.alloc("masksb", [128, 2048], BF16)
    wqa_b = A.alloc("wqa_b", [128, 512], BF16)
    wqb_b = A.alloc("wqb_b", [64, E], BF16)
    wkva_b = A.alloc("wkva_b", [128, 256], BF16)
    wkvk_b = A.alloc("wkvk_b", [128, 256], BF16)
    wkvv_b = A.alloc("wkvv_b", [128, 256], BF16)
    wout_pair = A.alloc("wout_pair", [128, 8 * E], BF16)
    bout_b = A.alloc("bout_b", [1, E], BF16)
    kn2max = A.alloc("kn2max", [128, 1], F32)
    kn2 = [A.alloc(f"kn2_{b}", [128, 1], F32) for b in range(2)]
    kn2m_bf = A.alloc("kn2m_bf", [128, 8], BF16)
    kmax2 = A.alloc("kmax2", [1, 1], F32)
    biasC = A.alloc("biasC", [128, 1], F32)
    qn2 = [A.alloc(f"qn2_{b}", [128, 1], F32) for b in range(2)]
    qn2m = A.alloc("qn2m", [128, 1], F32)
    qn2m_bf = A.alloc("qn2m_bf", [128, 8], BF16)
    c1 = A.alloc("c1", [1, 1], F32)
    c2 = A.alloc("c2", [1, 1], F32)
    c3 = A.alloc("c3", [1, 1], F32)
    c4 = A.alloc("c4", [1, 1], F32)
    eps_b = A.alloc("eps_b", [128, 1], F32)
    qrow_sb = A.alloc("qrow_sb", [1, 128], F32)
    krow_sb = A.alloc("krow_sb", [1, 128], F32)
    negC_bf = A.alloc("negC_bf", [1, 1], BF16)
    rt1 = A.alloc("rt1", [128, 512], F32)
    rt2 = A.alloc("rt2", [128, 512], F32)

    scratch0 = A.mark()
    xnT = [A.alloc(f"xnT{k}", [128, S], BF16) for k in range(8)]
    xbuf = [A.alloc(f"xbuf{b}", [128, E], F32) for b in range(2)]
    xn_tile = [A.alloc(f"xn_tile{b}", [128, E], BF16) for b in range(2)]
    xsq = A.alloc("xsq", [128, E], F32)
    lnsum = A.alloc("lnsum", [128, 1], F32)
    sq2 = A.alloc("sq2", [128, 1], F32)
    negm = A.alloc("negm", [128, 1], F32)
    m2v = A.alloc("m2v", [128, 1], F32)
    varv = A.alloc("varv", [128, 1], F32)
    lnv = A.alloc("lnv", [128, 1], F32)
    rstd = A.alloc("rstd", [128, 1], F32)
    kroped = [A.alloc(f"kroped{b}", [128, 256], BF16) for b in range(2)]
    ksq = A.alloc("ksq", [128, 256], BF16)
    wnat = [A.alloc(f"wnat{b}", [128, E], BF16) for b in range(4)]
    wstage = [A.alloc(f"wstage{b}", [128, E], BF16) for b in range(2)]
    wd_bf = [A.alloc(f"wd_bf{q}", [128, HID], BF16) for q in range(4)]
    pro_end = A.cur

    A.reset(scratch0)
    xnS = [A.alloc("xnS0", [128, 8 * 512], BF16)]
    wgB = [A.alloc(f"wgB{b}", [128, E], BF16) for b in range(2)]
    wuB = [A.alloc(f"wuB{b}", [128, E], BF16) for b in range(2)]
    wdB = [A.alloc(f"wdB{b}", [128, 2 * E], BF16) for b in range(2)]
    hT = A.alloc("hT", [128, NHD * 512], BF16)
    qroped = [A.alloc(f"qroped{b}", [128, E], BF16) for b in range(2)]
    qTc = [A.alloc(f"qTc{k}", [128, IC], BF16) for k in range(8)]
    Pt = [A.alloc(f"Pt{b}", [128, 1024], BF16) for b in range(4)]
    ot = [A.alloc(f"otp{p}", [128, IC], BF16) for p in range(8)]
    ot_stg = A.alloc("ot_stg", [64, IC], BF16)
    g_nat = [A.alloc(f"g_nat{t}", [128, E], BF16) for t in range(2 * TPC)]
    eF = [A.alloc(f"eF{b}", [128, 512], F32) for b in range(2)]
    eG = [A.alloc(f"eG{b}", [128, E], F32) for b in range(2)]
    gB2 = A.alloc("gB2", [128, 512], F32)
    jnk = A.alloc("jnk", [128, 8], F32)
    outb = [A.alloc(f"outb{b}", [128, E], F32) for b in range(2)]
    _u0 = A.mark()
    sgtmp = A.alloc("sgtmp", [128, E], F32)
    _u1 = A.mark()
    A.reset(_u0)
    recl = A.alloc("recl", [128, 256], F32)
    A.reset(_u0)
    qsq = A.alloc("qsq", [128, E], BF16)
    A.reset(_u1)
    Of32 = A.alloc("Of32", [64, 1024], F32)
    rec_rep = A.alloc("rec_rep", [64, 1024], F32)
    main_end = A.cur

    P = [nc.alloc_psum_tensor(f"P{i}", [128, 1024], F32) for i in range(2)]
    _p2_base = nc.psum_base
    P.append(nc.alloc_psum_tensor("P2", [128, 1024], F32))
    _after_p2 = nc.psum_base
    nc.psum_base = _p2_base
    QW = nc.alloc_psum_tensor("QW", [128, 1024], BF16)
    nc.psum_base = _after_p2
    _qx_base = nc.psum_base
    QX = nc.alloc_psum_tensor("QX", [128, 1024], BF16)
    QX2 = nc.alloc_psum_tensor("QX2", [128, 1024], BF16)
    nc.psum_base = _qx_base
    P3 = nc.alloc_psum_tensor("P3", [128, 1024], F32)

    dump_list = []
    dump_drams = {}
    FULLDBG = DEBUG_DUMPS is True
    if DEBUG_DUMPS:
        if FULLDBG:
            dbg1 = A.alloc("dbg1", [128, 1024], F32)
            dbg2 = A.alloc("dbg2", [128, 1024], F32)
            dbg3 = A.alloc("dbg3", [128, 1024], BF16)
        else:
            dbg1 = dbg2 = dbg3 = None
        dbg4 = A.alloc("dbg4", [128, 1], F32)
        dbg5 = A.alloc("dbg5", [1, 8], F32)
        dbg6 = A.alloc("dbg6", [1, 128], F32)
        build_nc.dbg = (dbg1, dbg2, dbg3, dbg4, dbg5, dbg6)
        def dreg(nm, sb_ap, shape, dtype):
            dump_drams[nm] = nc.dram_tensor(nm, shape, dtype, kind="ExternalOutput").ap()
            dump_list.append((nm, sb_ap))
        if FULLDBG:
            dreg("d_qk1T", qk1T[:, :], [128, S], BF16)
            dreg("d_kT0", kT0[:, :], [128, S], BF16)
            dreg("d_kT1", kT1[:, :], [128, S], BF16)
            dreg("d_Vp", Vp[:, :], [128, KVH * NT * 65], BF16)
            for k in range(8):
                dreg(f"d_qTc{k}", qTc[k][:, :], [128, IC], BF16)
        for hh in range(8):
            dreg(f"d_ot{hh}", ot[hh][:, :], [128, IC], BF16)
        dreg("d_xnS", xnS[0][:, :], [128, 8 * 512], BF16)
        if not FULLDBG:
            dreg("d_xnt_full", dxnt, [E, S], BF16)
            dreg("d_xin", dx, [S, E], F32)
            dreg("d_qTc0", qTc[0][:, :], [128, IC], BF16)
            dreg("d_qTc7", qTc[7][:, :], [128, IC], BF16)
            dreg("d_kT0", kT0[:, :], [128, S], BF16)
            dreg("d_Vp", Vp[:, :], [128, KVH * NT * 65], BF16)
            dreg("d_qk1T", qk1T[:, :], [128, S], BF16)
        dreg("d_hT", hT[:, :], [128, NHD * 512], BF16)
        for t in range(TPC):
            dreg(f"d_gnat{t}", g_nat[t][:, :], [128, E], F32)
        dreg("d_recrep", rec_rep[:, :], [64, 1024], BF16)
        dreg("d_reclbf", recl[:, :], [1, 1024], F32)
        dreg("d_Pt0", Pt[0][:, :], [128, 1024], BF16)
        dreg("d_Pt1", Pt[1][:, :], [128, 1024], BF16)
        if FULLDBG:
            dreg("d_dbg1", dbg1[:, :], [128, 1024], F32)
            dreg("d_dbg2", dbg2[:, :], [128, 1024], F32)
            dreg("d_dbg3", dbg3[:, :], [128, 1024], BF16)
        dreg("d_xnt_s", dxnt, [E, S], BF16)
        dreg("d_wgt_s", dwgt, [NHD * 128, E], BF16)
        dreg("d_wut_s", dwut, [NHD * 128, E], BF16)
        dreg("d_wdt_s", dwdt, [NHD * 128, E], BF16)

    lndbg = {}
    if DEBUG_LN:
        lndbg["g_x0"] = nc.dram_tensor("g_x0", [128, E], F32, kind="ExternalOutput").ap()
        lndbg["g_xn0"] = nc.dram_tensor("g_xn0", [128, E], BF16, kind="ExternalOutput").ap()
        for nm in ("g_lnsum", "g_sq2", "g_negm", "g_varv", "g_lnv", "g_rstd", "g_m2v"):
            lndbg[nm] = nc.dram_tensor(nm, [128, 1], F32, kind="ExternalOutput").ap()

    sems = {}
    sem_guards = []
    sem_names = ["T", "V", "SC", "SY", "GP"]
    for q, n in DMA_LANES.items():
        sem_names += [f"{q}{i}" for i in range(n)]
    for name in sem_names:
        g = nc.semaphore(name)
        h = g.__enter__()
        sem_guards.append(g)
        sems[name] = h

    def program(ctx):
        op, dma = ctx.op, ctx.dma
        TT = ALU

        def cast_load(name, out_ap, in_ap):
            dma("gpsimd", ("c", name), [], lambda e, o=out_ap, i=in_ap: e.dma_start(o, i))

        cast_load("ident", identb[:, :], dident)
        cast_load("ones", onesb[:, :], dones)
        cast_load("masks", masksb[:, :], dmasks)
        cast_load("wqa", wqa_b[:, :].rearrange("p (c k) -> p c k", k=64),
                  dwqa.rearrange("(c p) k -> p c k", p=128))
        cast_load("wqb", wqb_b[:, :], dwqb)
        cast_load("wkva", wkva_b[:, :].rearrange("p (c k) -> p c k", k=32),
                  dwkva.rearrange("(c p) k -> p c k", p=128))
        cast_load("wkvk", wkvk_b[64:96, :], dwkvk)
        cast_load("wkvv", wkvv_b[64:96, :], dwkvv)
        cast_load("wout", wout_pair[:, :].rearrange("p (c n) -> p c n", n=E),
                  dwoutp.rearrange("(c p) n -> p c n", p=128))
        cast_load("bout", bout_b[:, :], dbout)
        dma("sync", ("c", "cos"), [], lambda e: e.dma_start(
            cosb[:, :].rearrange("p (c k) -> p c k", k=32),
            dcos.rearrange("(c p) k -> p c k", p=128)))
        dma("sync", ("c", "sin"), [], lambda e: e.dma_start(
            sinb[:, :].rearrange("p (c k) -> p c k", k=32),
            dsin.rearrange("(c p) k -> p c k", p=128)))
        op("vector", ("vpms",), [], lambda e: e.memset(Vp[:, :], 1.0))
        op("vector", ("epsms",), [], lambda e: e.memset(eps_b[:, :], 1e-5))

        cvt_units = []

        def mk_w_unit(tag, src_ap, dst_ap, Hh, prev_wt, prev_wc, prev_ws):
            def dmau():
                b = Hh % 4
                dma("gpsimd", ("wl", tag, Hh),
                    [("wt", tag, Hh - 4) if Hh >= 4 else prev_wt],
                    lambda e: e.dma_start(
                        wnat[b][:, :], src_ap[Hh * 128:(Hh + 1) * 128, :]))

            def peu():
                b = Hh % 4
                for pchunk in range(8):
                    op("tensor", ("wt", tag, Hh) if pchunk == 7 else None,
                       [("wl", tag, Hh),
                        ("wc", tag, Hh - 1) if Hh >= 1 else prev_wc],
                       lambda e, pchunk=pchunk: e.transpose(
                           QW[:, pchunk * 128:(pchunk + 1) * 128],
                           wnat[b][:, pchunk * 128:(pchunk + 1) * 128], identb[:, :]))
                op("scalar", ("wc", tag, Hh),
                   [("wt", tag, Hh), ("ws", tag, Hh - 2) if Hh >= 2 else prev_ws],
                   lambda e: e.copy(wstage[Hh % 2][:, :], QW[:, :]))
                dma("sync", ("ws", tag, Hh), [("wc", tag, Hh)],
                    lambda e: e.dma_start(
                        dst_ap[Hh * 128:(Hh + 1) * 128, :], wstage[Hh % 2][:, :]))
            return (dmau, peu)

        for Hh in range(NHD):
            cvt_units.append(mk_w_unit("wg", dwg, dwgt, Hh, None, None, None))
        for Hh in range(NHD):
            cvt_units.append(mk_w_unit(
                "wu", dwu, dwut, Hh, ("wt", "wg", NHD - 1),
                ("wc", "wg", NHD - 1), ("ws", "wg", NHD - 1)))

        def mk_wdl_unit(h4):
            def dmau():
                for q in range(2):
                    dma("gpsimd", ("wdl", h4, q),
                        [("wdt", h4 - 2, NHD - 1) if h4 >= 2 else None],
                        lambda e, q=q: e.dma_start(
                            wd_bf[q + 2 * (h4 % 2)][:, :],
                            dwd[(h4 * 2 + q) * 128:(h4 * 2 + q + 1) * 128, :]))
            return (dmau, None)

        def mk_wd_unit(h4, Hh):
            def peu():
                for q in range(2):
                    op("tensor", ("wdt", h4, Hh) if q == 1 else None,
                       [("wdl", h4, 1), ("wc", "wu", NHD - 1),
                        ("wdc", h4, Hh - 1) if Hh >= 1 else (("wdc", h4 - 1, NHD - 1) if h4 >= 1 else None)],
                       lambda e, q=q: e.transpose(
                           QW[:, q * 128:(q + 1) * 128],
                           wd_bf[q + 2 * (h4 % 2)][:, Hh * 128:(Hh + 1) * 128], identb[:, :]))
                wdc_w = [("wdt", h4, Hh)]
                if Hh >= 2:
                    wdc_w.append(("wds", h4, Hh - 2))
                elif h4 >= 1:
                    wdc_w.append(("wds", h4 - 1, NHD - 1))
                else:
                    wdc_w.append(("ws", "wu", NHD - 1))
                op("scalar", ("wdc", h4, Hh), wdc_w,
                   lambda e: e.copy(wstage[Hh % 2][:, 0:256], QW[:, 0:256]))
                dma("sync", ("wds", h4, Hh), [("wdc", h4, Hh)],
                    lambda e: e.dma_start(
                        dwdt[Hh * 128:(Hh + 1) * 128, h4 * 256:(h4 + 1) * 256],
                        wstage[Hh % 2][:, 0:256]))
            return (None, peu)

        for h4 in range(4):
            cvt_units.append(mk_wdl_unit(h4))
            for Hh in range(NHD):
                cvt_units.append(mk_wd_unit(h4, Hh))

        cvt_pos = {"d": 0, "p": 0}

        def pop_cvt(n):
            for _ in range(n):
                if cvt_pos["p"] >= len(cvt_units):
                    return
                while cvt_pos["d"] < min(cvt_pos["p"] + 4, len(cvt_units)):
                    d = cvt_units[cvt_pos["d"]][0]
                    if d is not None:
                        d()
                    cvt_pos["d"] += 1
                pe = cvt_units[cvt_pos["p"]][1]
                if pe is not None:
                    pe()
                cvt_pos["p"] += 1

        for it in range(NT):
            b = it % 2
            dma("sync", ("x", it), [("xn", it - 2) if it >= 2 else ("c", "bout"),
                                    ("gx0",) if (DEBUG_LN and it == 2) else None],
                lambda e, it=it, b=b: e.dma_start(xbuf[b][:, :], dx[it * 128:(it + 1) * 128, :]))
            if DEBUG_LN and it == 0:
                dma("sync", ("gx0",), [("x", 0)],
                    lambda e: e.dma_start(lndbg["g_x0"], xbuf[0][:, :]))
            st_guard = [("gst", k) for k in range(5)] if (DEBUG_LN and it == 1) else []
            op("vector", None, [("x", it)] + st_guard,
               lambda e, b=b: e.reduce_sum(lnsum[:, :], xbuf[b][:, :], axis=AX.X))
            op("vector", None, [],
               lambda e, b=b: e.tensor_mul(xsq[:, :], xbuf[b][:, :], xbuf[b][:, :]))
            op("vector", None, [], lambda e: e.tensor_scalar_mul(negm[:, :], lnsum[:, :], -1.0 / E))
            op("vector", None, [], lambda e: e.reduce_sum(sq2[:, :], xsq[:, :], axis=AX.X))
            op("vector", None, [], lambda e: e.tensor_mul(m2v[:, :], negm[:, :], negm[:, :]))
            op("vector", None, [],
               lambda e: e.scalar_tensor_tensor(varv[:, :], sq2[:, :], 1.0 / E, m2v[:, :], TT.mult, TT.subtract))
            op("vector", None, [],
               lambda e: e.scalar_tensor_tensor(varv[:, :], sq2[:, :], 1.0 / E, m2v[:, :], TT.mult, TT.subtract))
            op("vector", ("var", it), [],
               lambda e: e.scalar_tensor_tensor(varv[:, :], sq2[:, :], 1.0 / E, m2v[:, :], TT.mult, TT.subtract))
            if DEBUG_LN and it == 0:
                for k, (nm, ap) in enumerate((("g_lnsum", lnsum), ("g_sq2", sq2),
                                              ("g_negm", negm), ("g_varv", varv),
                                              ("g_m2v", m2v))):
                    dma("sync", ("gst", k), [("var", 0)],
                        lambda e, nm=nm, ap=ap: e.dma_start(lndbg[nm], ap[:, :]))
            ln_guard = [("gla", 0), ("gla", 1)] if (DEBUG_LN and it == 1) else []
            op("scalar", None, [("var", it), ("epsms",)] + ln_guard, lambda e: e.activation(lnv[:, :], varv[:, :], AF.Ln, bias=eps_b[:, 0:1]))
            op("scalar", None, [], lambda e: e.activation(rstd[:, :], lnv[:, :], AF.Exp, scale=-0.5))
            op("scalar", ("rstd", it), [], lambda e: e.activation(rstd[:, :], lnv[:, :], AF.Exp, scale=-0.5))
            if DEBUG_LN and it == 0:
                for k, (nm, ap) in enumerate((("g_lnv", lnv), ("g_rstd", rstd))):
                    dma("sync", ("gla", k), [("rstd", 0)],
                        lambda e, nm=nm, ap=ap: e.dma_start(lndbg[nm], ap[:, :]))
            op("vector", ("xn", it), [("rstd", it), ("xp", it - 2) if it >= 2 else None,
                                      ("gxn0",) if (DEBUG_LN and it == 2) else None],
               lambda e, b=b: e.tensor_scalar(xn_tile[b][:, :], xbuf[b][:, :], negm[:, :], rstd[:, :], TT.add, TT.mult))
            pop_cvt(1)
            if DEBUG_LN and it == 0:
                dma("sync", ("gxn0",), [("xn", 0)],
                    lambda e: e.dma_start(lndbg["g_xn0"], xn_tile[0][:, :]))
            ps = QX if b == 0 else QX2
            for k in range(8):
                op("tensor", ("xp", it) if k == 7 else None,
                   [("xn", it), ("c", "ident"), ("xncp", it - 2) if it >= 2 else None],
                   lambda e, k=k, b=b, ps=ps: e.transpose(
                       ps[:, k * 128:(k + 1) * 128], xn_tile[b][:, k * 128:(k + 1) * 128], identb[:, :]))
            for k in range(8):
                op("scalar", ("xncp", it) if k == 7 else None, [("xp", it)],
                   lambda e, k=k, it=it, ps=ps: e.copy(xnT[k][:, it * 128:(it + 1) * 128], ps[:, k * 128:(k + 1) * 128]))
        for k in range(8):
            dma("sync", ("xnts", k), [("xncp", NT - 1)],
                lambda e, k=k: e.dma_start(dxnt[k * 128:(k + 1) * 128, :], xnT[k][:, :]))

        NSC = S // 512
        for sc in range(NSC):
            pop_cvt(2)
            pbuf = P[0][:, 0:512] if sc % 2 == 0 else P[0][:, 512:1024]
            for e_ in range(8):
                op("tensor", None,
                   [("xncp", min(NT - 1, 4 * sc + 3)), ("c", "wqa"),
                    ("qk1cp", sc - 2) if sc >= 2 else None],
                   lambda e, e_=e_, sc=sc, pbuf=pbuf: e.matmul(
                       pbuf[0:64, :], wqa_b[:, e_ * 64:(e_ + 1) * 64],
                       xnT[e_][:, sc * 512:(sc + 1) * 512], start=(e_ == 0), stop=(e_ == 7)))
            for e_ in range(8):
                op("tensor", ("qk1", sc) if e_ == 7 else None, [("c", "wkva")],
                   lambda e, e_=e_, sc=sc, pbuf=pbuf: e.matmul(
                       pbuf[64:96, :], wkva_b[:, e_ * 32:(e_ + 1) * 32],
                       xnT[e_][:, sc * 512:(sc + 1) * 512], start=(e_ == 0), stop=(e_ == 7)))
            op("vector", None, [("qk1", sc)],
               lambda e, sc=sc, pbuf=pbuf: e.tensor_copy(qk1T[0:64, sc * 512:(sc + 1) * 512], pbuf[0:64, :]))
            op("vector", ("qk1cp", sc), [],
               lambda e, sc=sc, pbuf=pbuf: e.tensor_copy(qk1T[64:96, sc * 512:(sc + 1) * 512], pbuf[64:96, :]))

        def emit_kv_mm(it):
            b = it % 2
            pkv = P[1][:, 0:512] if b == 0 else P[1][:, 512:1024]
            op("tensor", None,
               [("qk1cp", it // 4), ("c", "wkvk"), ("vkv", it - 2) if it >= 2 else None],
               lambda e, it=it, pkv=pkv: e.matmul(
                   pkv[:, 0:256], qk1T[64:96, it * 128:(it + 1) * 128], wkvk_b[64:96, :], start=True, stop=True))
            op("tensor", ("kv", it), [("c", "wkvv")],
               lambda e, it=it, pkv=pkv: e.matmul(
                   pkv[:, 256:512], qk1T[64:96, it * 128:(it + 1) * 128], wkvv_b[64:96, :], start=True, stop=True))

        emit_kv_mm(0)
        for it in range(NT):
            b = it % 2
            pkv = P[1][:, 0:512] if b == 0 else P[1][:, 512:1024]
            if it + 1 < NT:
                emit_kv_mm(it + 1)
            pop_cvt(3)
            kv_guard = ("kv", it + 1) if it + 1 < NT else ("ktp", NT - 2)
            x1 = pkv[:, 0:256].rearrange("p (g u t) -> p g u t", u=2, t=32)[:, :, 0:1, :]
            x2 = pkv[:, 0:256].rearrange("p (g u t) -> p g u t", u=2, t=32)[:, :, 1:2, :]
            cosk = cosb[:, it * 32:(it + 1) * 32].rearrange("p (g o t) -> p g o t", g=1, o=1).broadcast_to([128, 4, 1, 32])
            sink = sinb[:, it * 32:(it + 1) * 32].rearrange("p (g o t) -> p g o t", g=1, o=1).broadcast_to([128, 4, 1, 32])
            t1 = rt1[:, 0:128].rearrange("p (g o t) -> p g o t", o=1, t=32)
            t2 = rt2[:, 0:128].rearrange("p (g o t) -> p g o t", o=1, t=32)
            kr1 = kroped[b][:, :].rearrange("p (g u t) -> p g u t", u=2, t=32)[:, :, 0:1, :]
            kr2 = kroped[b][:, :].rearrange("p (g u t) -> p g u t", u=2, t=32)[:, :, 1:2, :]
            op("vector", None, [kv_guard, ("c", "cos"), ("c", "sin"),
                                ("ktp", it - 2) if it >= 2 else None],
               lambda e, x1=x1, cosk=cosk, t1=t1: e.tensor_tensor(t1, x1, cosk, TT.mult))
            op("vector", None, [], lambda e, x2=x2, sink=sink, t2=t2: e.tensor_tensor(t2, x2, sink, TT.mult))
            op("vector", None, [], lambda e, kr1=kr1, t1=t1, t2=t2: e.tensor_tensor(kr1, t1, t2, TT.subtract))
            op("vector", None, [], lambda e, x1=x1, sink=sink, t1=t1: e.tensor_tensor(t1, x1, sink, TT.mult))
            op("vector", None, [], lambda e, x2=x2, cosk=cosk, t2=t2: e.tensor_tensor(t2, x2, cosk, TT.mult))
            op("vector", ("ropek", it), [], lambda e, kr2=kr2, t1=t1, t2=t2: e.tensor_tensor(kr2, t1, t2, TT.add))
            for g in range(4):
                op("vector", None, [("vpms",)],
                   lambda e, g=g, it=it, pkv=pkv: e.tensor_copy(
                       Vp[:, (g * NT + it) * 65:(g * NT + it) * 65 + 64], pkv[:, 256 + g * 64:256 + (g + 1) * 64]))
            op("vector", ("vkv", it), [], lambda e: e.tensor_copy(c1[:, :], c2[:, :]))
            for half in range(2):
                op("tensor", ("ktp", it) if half == 1 else None,
                   [("ropek", it), ("ktcp", it - 1) if it >= 1 else None],
                   lambda e, half=half, b=b: e.transpose(
                       QX2[:, half * 128:(half + 1) * 128], kroped[b][:, half * 128:(half + 1) * 128], identb[:, :]))
            for half in range(2):
                kdst = kT0 if half == 0 else kT1
                op("scalar", ("ktcp", it) if half == 1 else None, [("ktp", it)],
                   lambda e, half=half, it=it, kdst=kdst: e.copy(
                       kdst[:, it * 128:(it + 1) * 128], QX2[:, half * 128:(half + 1) * 128]))
        while cvt_pos["p"] < len(cvt_units):
            pop_cvt(1)


        barrier_evs = ([("wdt", 3, NHD - 1), ("ktp", NT - 1), ("qk1", NSC - 1),
                        ("vkv", NT - 1), ("ropek", NT - 1), ("xn", NT - 1),
                        ("wdc", 3, NHD - 1), ("xncp", NT - 1), ("ktcp", NT - 1),
                        ("wdl", 3, 1)]
                       + [("wds", 3, Hh) for Hh in range(NHD - 8, NHD)]
                       + [("xnts", k) for k in range(8)])
        for engname in ("sync", "gpsimd", "tensor", "vector", "scalar"):
            own = SEM_OF[engname]
            ctx.op(engname, ("mb", engname),
                   [ev for ev in barrier_evs if ctx.events[ev][0] != own],
                   lambda e: e.nop())

        NLD = NHD // 2

        def emit_xns(ic):
            for k in range(8):
                dma("sync", ("xnl", ic) if k == 7 else None,
                    [("xnts", 7), ("gu", ic - 1, NHD - 1) if ic >= 1 else None],
                    lambda e, k=k, ic=ic: e.dma_start(
                        xnS[0][:, k * 512:(k + 1) * 512],
                        dxnt[k * 128:(k + 1) * 128, ic * 512:(ic + 1) * 512]))

        def emit_F1_dma(ic1, hd):
            b = hd % 2
            wl_w = [("ws", "wg", NHD - 1)]
            if hd >= 2:
                wl_w.append(("gu", ic1, hd - 2))
            elif ic1 >= 1:
                wl_w.append(("gu", ic1 - 1, NHD - 2 + hd))
            dma("sync", ("wgl", ic1, hd), wl_w,
                lambda e, hd=hd, b=b: e.dma_start(wgB[b][:, :], dwgt[hd * 128:(hd + 1) * 128, :]))
            dma("sync", ("wul", ic1, hd), [("ws", "wu", NHD - 1)],
                lambda e, hd=hd, b=b: e.dma_start(wuB[b][:, :], dwut[hd * 128:(hd + 1) * 128, :]))

        def emit_F1_unit(ic1, hd, part):
            b = hd % 2
            psG = P[1][:, 0:512]
            psU = P[1][:, 512:1024]
            if part == 0:
                gu_w = [("wgl", ic1, hd), ("wul", ic1, hd), ("xnl", ic1)]
                if hd >= 1:
                    gu_w.append(("gb2", ic1, hd - 1))
                elif ic1 >= 1:
                    gu_w.append(("rope", ic1 - 1, TPC - 1))
                for e_ in range(8):
                    op("tensor", None, gu_w if e_ == 0 else [],
                       lambda e, e_=e_, psG=psG: e.matmul(
                           psG, wgB[b][:, e_ * 128:(e_ + 1) * 128], xnS[0][:, e_ * 512:(e_ + 1) * 512],
                           start=(e_ == 0), stop=(e_ == 7)))
                return
            for e_ in range(8):
                op("tensor", ("gu", ic1, hd) if e_ == 7 else None,
                   ([("h", ic1, hd - 1)] if hd >= 1 else []) if e_ == 0 else [],
                   lambda e, e_=e_, psU=psU: e.matmul(
                       psU, wuB[b][:, e_ * 128:(e_ + 1) * 128], xnS[0][:, e_ * 512:(e_ + 1) * 512],
                       start=(e_ == 0), stop=(e_ == 7)))
            sg_w = [("gu", ic1, hd)]
            if hd >= 2:
                sg_w.append(("h", ic1, hd - 2))
            op("scalar", ("sg", ic1, hd), sg_w,
               lambda e, psG=psG, b=b: e.activation(eF[b][:, :], psG, AF.Tanh, scale=DEQ * 0.5))
            h_w = [("sg", ic1, hd)]
            if hd == 0 and ic1 >= 1:
                h_w.append(("dn", ic1 - 1, 1, 1, NLD - 1))
            op("vector", None, h_w,
               lambda e, b=b: e.tensor_scalar_add(eF[b][:, :], eF[b][:, :], 1.0))
            op("vector", None, [], lambda e: e.tensor_copy(jnk[:, 0:1], jnk[:, 1:2]))
            op("vector", ("gb2", ic1, hd), [],
               lambda e, psG=psG, b=b: e.scalar_tensor_tensor(
                   gB2[:, :], psG, DEQ * DEQ * 0.5, eF[b][:, :], TT.mult, TT.mult))
            op("vector", None, [], lambda e: e.tensor_copy(jnk[:, 0:1], jnk[:, 1:2]))
            op("vector", ("h", ic1, hd), [],
               lambda e, hd=hd, psU=psU: e.tensor_mul(
                   hT[:, hd * 512:(hd + 1) * 512], gB2[:, :], psU))

        wdBv = [wdB[0][:, 0:1024], wdB[0][:, 1024:2048],
                wdB[1][:, 0:1024], wdB[1][:, 1024:2048]]

        def emit_F2_dma(ic1, h2, seg, ld):
            b = ld % 4
            wdl_w = [("wds", 3, NHD - 1)]
            q4 = 2 * h2 + seg
            if ld >= 4:
                wdl_w.append(("dn", ic1, h2, seg, ld - 4))
            elif q4 >= 1:
                wdl_w.append(("dn", ic1, (q4 - 1) // 2, (q4 - 1) % 2, NLD - 4 + ld))
            elif ic1 >= 1:
                wdl_w.append(("dn", ic1 - 1, 1, 1, NLD - 4 + ld))
            dma("sync", ("wdl2", ic1, h2, seg, ld), wdl_w,
                lambda e, ld=ld, b=b, seg=seg: e.dma_start(
                    wdBv[b].rearrange("p (c k) -> p c k", k=512),
                    dwdt[ld * 256:(ld + 1) * 256, seg * 512:(seg + 1) * 512]
                    .rearrange("(c p) k -> p c k", p=128)))

        def emit_F2_unit(ic1, h2, seg, ld):
            b = ld % 4
            q4 = 2 * h2 + seg
            psD2 = [P[1][:, 0:512], P[1][:, 512:1024]]
            dn_w = [("wdl2", ic1, h2, seg, ld)]
            if ld == 0:
                if q4 == 0:
                    dn_w += [("h", ic1, NHD - 1)]
                else:
                    dn_w += [("gs", ic1, (q4 - 1) // 2, (q4 - 1) % 2)]
            for hd2 in range(2):
                hd = 2 * ld + hd2
                for tl in range(2):
                    t = 2 * h2 + tl
                    op("tensor",
                       ("dn", ic1, h2, seg, ld) if (hd2 == 1 and tl == 1) else None,
                       dn_w if (hd2 == 0 and tl == 0) else [],
                       lambda e, hd2=hd2, seg=seg, hd=hd, b=b, t=t, tl=tl: e.matmul(
                           psD2[tl],
                           hT[:, hd * 512 + t * 128:hd * 512 + (t + 1) * 128],
                           wdBv[b][:, hd2 * 512:(hd2 + 1) * 512],
                           start=(ld == 0 and hd2 == 0), stop=(ld == NLD - 1 and hd2 == 1)))
            if ld == NLD - 1:
                gb = (ic1 % 2) * TPC
                gs_w = [("dn", ic1, h2, seg, NLD - 1)]
                for tl in range(2):
                    t = 2 * h2 + tl
                    op("scalar", ("ge", ic1, t, seg) if tl == 1 else None,
                       gs_w if tl == 0 else [],
                       lambda e, tl=tl: e.activation(
                           eG[tl][:, 0:512], psD2[tl], AF.Tanh, scale=DEQ * 0.5))
                gv_w = [("ge", ic1, 2 * h2 + 1, seg)]
                if ic1 >= 2:
                    gv_w.append(("comb", ic1 - 2, TPC - 1))
                for tl in range(2):
                    op("vector", None, gv_w if tl == 0 else [],
                       lambda e, tl=tl: e.tensor_scalar_add(eG[tl][:, 0:512], eG[tl][:, 0:512], 1.0))
                for tl in range(2):
                    t = 2 * h2 + tl
                    op("vector", ("gs", ic1, h2, seg) if tl == 1 else None, [],
                       lambda e, t=t, tl=tl, gb=gb, seg=seg: e.scalar_tensor_tensor(
                           g_nat[gb + t][:, seg * 512:(seg + 1) * 512], psD2[tl],
                           DEQ * 0.5, eG[tl][:, 0:512], TT.mult, TT.mult))

        def emit_W_tile(ic, t):
            it = ic * TPC + t
            gb = (ic % 2) * TPC
            xr_w = [("ost", ic, t - 2) if t >= 2 else (("ost", ic - 1, t + 2) if ic > 0 else None),
                    ("comb", ic, t - 2) if t >= 2 else (("comb", ic - 1, t + 2) if ic > 0 else None)]
            dma("gpsimd", ("xr", ic, t), xr_w,
                lambda e, it=it, t=t: e.dma_start(outb[t % 2][:, :], dx[it * 128:(it + 1) * 128, :]))
            wo_w = [("c", "wout"), ("c", "bout"), ("ots", ic, 0)]
            if t == 0:
                wo_w.append(("exp", ic, 7, TPC * ic + TPC - 1))
            else:
                wo_w.append(("comb", ic, t - 1))
            for pr in range(8):
                pr_w = wo_w if (pr == 0) else [("ots", ic, pr)]
                for seg in range(2):
                    op("tensor", None, pr_w if seg == 0 else [],
                       lambda e, pr=pr, seg=seg, t=t: e.matmul(
                           P[0][:, seg * 512:(seg + 1) * 512], ot[pr][:, t * 128:(t + 1) * 128],
                           wout_pair[:, pr * E + seg * 512:pr * E + (seg + 1) * 512],
                           start=(pr == 0), stop=False))
            for seg in range(2):
                op("tensor", ("wo", ic, t) if seg == 1 else None, [],
                   lambda e, seg=seg: e.matmul(
                       P[0][:, seg * 512:(seg + 1) * 512], onesb[0:1, :],
                       bout_b[0:1, seg * 512:(seg + 1) * 512], start=False, stop=True))
            comb_w = [("wo", ic, t), ("xr", ic, t),
                      ("gs", ic, t // 2, 0), ("gs", ic, t // 2, 1)]
            if t == 0:
                comb_w.append(("rrep", ic, 7))
            op("vector", None, comb_w,
               lambda e, t=t, gb=gb: e.tensor_mul(sgtmp[:, :], P[0][:, :], g_nat[gb + t][:, :]))
            op("vector", ("comb", ic, t), [],
               lambda e, t=t: e.tensor_add(outb[t % 2][:, :], sgtmp[:, :], outb[t % 2][:, :]))
            dma("gpsimd", ("ost", ic, t), [("comb", ic, t)],
                lambda e, it=it, t=t: e.dma_start(dout[it * 128:(it + 1) * 128, :], outb[t % 2][:, :]),
                sem="OS")

        def emit_Q_tile(ic, t):
            it = ic * TPC + t
            mm_waits = [("qk1cp", NSC - 1), ("c", "wqb")]
            if t == 0:
                mm_waits.append(("gs", ic, 1, 1))
            else:
                mm_waits.append(("rope", ic, t - 1))
            for seg in range(2):
                op("tensor", ("mmq", ic, t) if seg == 1 else None, mm_waits if seg == 0 else [],
                   lambda e, it=it, seg=seg: e.matmul(
                       P[1][:, seg * 512:(seg + 1) * 512], qk1T[0:64, it * 128:(it + 1) * 128],
                       wqb_b[:, seg * 512:(seg + 1) * 512], start=True, stop=True))
            if ic == 0:
                op("tensor", ("qdg", ic, t), [],
                   lambda e: e.matmul(P[2][64:128, 0:64], onesb[0:1, 0:64],
                                      bout_b[0:1, 0:64], start=True, stop=True))
            pq = P[1]
            x1 = pq[:, :].rearrange("p (h u t) -> p h u t", u=2, t=32)[:, :, 0:1, :]
            x2 = pq[:, :].rearrange("p (h u t) -> p h u t", u=2, t=32)[:, :, 1:2, :]
            cosq = cosb[:, it * 32:(it + 1) * 32].rearrange("p (g o t) -> p g o t", g=1, o=1).broadcast_to([128, 16, 1, 32])
            sinq = sinb[:, it * 32:(it + 1) * 32].rearrange("p (g o t) -> p g o t", g=1, o=1).broadcast_to([128, 16, 1, 32])
            t1 = rt1[:, :].rearrange("p (h o t) -> p h o t", o=1, t=32)
            t2 = rt2[:, :].rearrange("p (h o t) -> p h o t", o=1, t=32)
            qb = t % 2
            qr1 = qroped[qb][:, :].rearrange("p (h u t) -> p h u t", u=2, t=32)[:, :, 0:1, :]
            qr2 = qroped[qb][:, :].rearrange("p (h u t) -> p h u t", u=2, t=32)[:, :, 1:2, :]
            rope_w = [("qdg", ic, t) if ic == 0 else ("wo", ic - 1, t)]
            if t >= 2:
                rope_w.append(("qxp", ic, t - 2))
            elif ic > 0:
                rope_w.append(("qxp", ic - 1, t + 2))
            op("vector", None, rope_w, lambda e, x1=x1, cosq=cosq, t1=t1: e.tensor_tensor(t1, x1, cosq, TT.mult))
            op("vector", None, [], lambda e, x2=x2, sinq=sinq, t2=t2: e.tensor_tensor(t2, x2, sinq, TT.mult))
            op("vector", None, [], lambda e, qr1=qr1, t1=t1, t2=t2: e.tensor_tensor(qr1, t1, t2, TT.subtract))
            op("vector", None, [], lambda e, x1=x1, sinq=sinq, t1=t1: e.tensor_tensor(t1, x1, sinq, TT.mult))
            op("vector", None, [], lambda e, x2=x2, cosq=cosq, t2=t2: e.tensor_tensor(t2, x2, cosq, TT.mult))
            op("vector", ("rope", ic, t), [], lambda e, qr2=qr2, t1=t1, t2=t2: e.tensor_tensor(qr2, t1, t2, TT.add))

        def emit_qxp(ic, t):
            qb = t % 2
            psx = QX if t % 2 == 0 else QX2
            xp_w = [("rope", ic, t)]
            if t >= 2:
                xp_w.append(("qtcp", ic, t - 2))
            elif ic > 0:
                xp_w.append(("qtcp", ic - 1, TPC - 1))
            if t == 0 and ic > 0:
                xp_w.append(("exp", ic - 1, 7, TPC * ic - 1))
            for k in range(8):
                op("tensor", ("qxp", ic, t) if k == 7 else None, xp_w if k == 0 else [],
                   lambda e, k=k, qb=qb, psx=psx: e.transpose(
                       psx[:, k * 128:(k + 1) * 128], qroped[qb][:, k * 128:(k + 1) * 128], identb[:, :]))
            qtcp_w = [("qxp", ic, t)]
            if ic > 0:
                qtcp_w.append(("qk", ic - 1, 7, TPC * ic - 1))
            for k in range(8):
                op("scalar", ("qtcp", ic, t) if k == 7 else None, qtcp_w if k == 0 else [],
                   lambda e, k=k, t=t, psx=psx: e.copy(
                       qTc[k][:, t * 128:(t + 1) * 128], psx[:, k * 128:(k + 1) * 128]))

        def emit_A_step(ic, p, jt, grs, jtlast):
            b = jt % 4
            psS = P3 if jt % 2 == 0 else P[0]
            for seg in range(2):
                qk_w = []
                if seg == 0:
                    qk_w = [("qtcp", ic, TPC - 1), ("ktcp", NT - 1), ("bias", ic)]
                    if jt >= 2:
                        qk_w.append(("exp", ic, p, jt - 2))
                    elif p > 0:
                        qk_w.append(("exp", ic, p - 1, jtlast - 1 + jt))
                    elif ic > 0:
                        qk_w.append(("exp", ic - 1, 7, jtlast - TPC - 1 + jt))
                    if jt == 1 and p == 0 and ic > 0:
                        qk_w.append(("comb", ic - 1, TPC - 1))
                ktsrc = kT0 if grs[seg] < 2 else kT1
                op("tensor", ("qk", ic, p, jt) if seg == 1 else None, qk_w,
                   lambda e, p=p, jt=jt, seg=seg, ktsrc=ktsrc, psS=psS: e.matmul(
                       psS[:, seg * 512:(seg + 1) * 512],
                       ktsrc[seg * 64:seg * 64 + 64, jt * 128:(jt + 1) * 128],
                       qTc[p][seg * 64:(seg + 1) * 64, :], start=True, stop=True))
            m = jt - TPC * ic
            exp_w = [("qk", ic, p, jt)]
            if jt >= 4:
                exp_w.append(("pv", ic, p, jt - 4))
            elif p > 0:
                exp_w.append(("pv", ic, p - 1, jtlast - 3 + jt))
            elif ic > 0:
                exp_w.append(("pv", ic - 1, 7, jtlast - TPC - 3 + jt))
            op("scalar", ("exp", ic, p, jt), exp_w,
               lambda e, b=b, psS=psS: e.activation(Pt[b][:, :], psS[:, :], AF.Exp, scale=SCALE))
            if m >= 0:
                for seg in range(2):
                    op("vector", ("mask", ic, p, jt, seg),
                       [("exp", ic, p, jt), ("c", "masks")],
                       lambda e, seg=seg, b=b, m=m: e.tensor_mul(
                           Pt[b][:, seg * 512:(seg + 1) * 512], Pt[b][:, seg * 512:(seg + 1) * 512],
                           masksb[:, m * 512:(m + 1) * 512]))

        def emit_pv(ic, p, jt, grs, jtlast):
            b = jt % 4
            m = jt - TPC * ic
            for seg in range(2):
                dep = ("mask", ic, p, jt, seg) if m >= 0 else ("exp", ic, p, jt)
                pv_w = [dep]
                if seg == 0:
                    pv_w.append(("vkv", NT - 1))
                    if jt == 0 and p > 0:
                        pv_w += [("ocp", ic, p - 1), ("lcp", ic, p - 1)]
                lbl = None
                if seg == 1:
                    lbl = ("pv", ic, p, jt)
                elif jt == jtlast:
                    lbl = ("pv0", ic, p)
                op("tensor", lbl, pv_w,
                   lambda e, seg=seg, b=b, gr=grs[seg], jt=jt: e.matmul(
                       P[2][0:65, seg * 512:(seg + 1) * 512],
                       Vp[:, (gr * NT + jt) * 65:(gr * NT + jt + 1) * 65],
                       Pt[b][:, seg * 512:(seg + 1) * 512],
                       start=(jt == 0), stop=(jt == jtlast)))

        def emit_A_tailA(ic, p, grs, jtlast):
            for jt in range(max(0, jtlast - 3), jtlast + 1):
                emit_pv(ic, p, jt, grs, jtlast)
            for q in range(4):
                op("scalar", ("lcp0", ic, p) if q == 1 else
                   (("lcp", ic, p) if q == 3 else None),
                   [("pv0", ic, p),
                    ("comb", ic - 1, TPC - 1) if (p == 0 and ic > 0) else None,
                    ("rrep", ic, p - 1) if p > 0 else
                    (("rrep", ic - 1, 7) if ic > 0 else None)] if q == 0 else
                   ([("pv", ic, p, jtlast)] if q == 2 else []),
                   lambda e, q=q: e.copy(recl[q * 32:q * 32 + 1, :],
                                         P[2][64:65, q * 256:(q + 1) * 256]))
            op("scalar", None, [("pv0", ic, p)],
               lambda e: e.copy(Of32[:, 0:512], P[2][0:64, 0:512]))
            op("scalar", ("ocp", ic, p), [("pv", ic, p, jtlast)],
               lambda e: e.copy(Of32[:, 512:1024], P[2][0:64, 512:1024]))

        def emit_A_tailB(ic, p, grs, jtlast):
            op("vector", ("rec", ic, p), [("lcp0", ic, p), ("lcp", ic, p)],
               lambda e: e.reciprocal(recl[:, :], recl[:, :]))
            for q in range(4):
                dma("gpsimd", ("rst", ic, p, q), [("rec", ic, p)] if q == 0 else [],
                    lambda e, q=q: e.dma_start(
                        drecl[0:1, q * 256:(q + 1) * 256], recl[q * 32:q * 32 + 1, :]))
            dma("gpsimd", ("rrep", ic, p),
                [("rst", ic, p, 0), ("rst", ic, p, 1), ("rst", ic, p, 2),
                 ("rst", ic, p, 3), ("ot", ic, p - 1) if p > 0 else
                 (("ot", ic - 1, 7) if ic > 0 else None)],
                lambda e: e.dma_start(rec_rep[:, :],
                                      drecl.broadcast_to([64, 1024])))
            op("vector", None, [("rrep", ic, p), ("ots", ic, p - 1) if p >= 1 else
                                (("ots", ic - 1, 7) if ic > 0 else None)],
               lambda e, p=p: e.tensor_mul(ot[p][0:64, :], Of32[:, 0:512], rec_rep[:, 0:512]))
            op("vector", ("ot", ic, p), [],
               lambda e, p=p: e.tensor_mul(ot_stg[:, :], Of32[:, 512:1024], rec_rep[:, 512:1024]))
            dma("gpsimd", ("ots", ic, p), [("ot", ic, p)],
                lambda e, p=p: e.dma_start(ot[p][64:128, :], ot_stg[:, :]))

        emit_xns(0)
        for hd in range(NHD):
            emit_F1_dma(0, hd)
            emit_F1_unit(0, hd, 0)
            emit_F1_unit(0, hd, 1)
        for h2 in range(2):
            for seg in range(2):
                for ld in range(NLD):
                    emit_F2_dma(0, h2, seg, ld)
                    emit_F2_unit(0, h2, seg, ld)

        for ic in range(NIC):
            jtlast = TPC * ic + TPC - 1
            if ic + 1 < NIC:
                emit_xns(ic + 1)
            for t in range(TPC):
                emit_Q_tile(ic, t)
                if ic > 0:
                    emit_W_tile(ic - 1, t)
                if t >= 1:
                    emit_qxp(ic, t - 1)
            emit_qxp(ic, TPC - 1)
            op("scalar", ("bias", ic), [], lambda e: e.copy(c3[:, :], c4[:, :]))

            f_units = []
            if ic + 1 < NIC:
                for hd in range(NHD):
                    f_units.append((lambda hd=hd, ic1=ic + 1: emit_F1_dma(ic1, hd),
                                    lambda hd=hd, ic1=ic + 1: emit_F1_unit(ic1, hd, 0)))
                    f_units.append((None,
                                    lambda hd=hd, ic1=ic + 1: emit_F1_unit(ic1, hd, 1)))
                for h2 in range(2):
                    for seg in range(2):
                        for ld in range(NLD):
                            f_units.append((lambda h2=h2, seg=seg, ld=ld, ic1=ic + 1: emit_F2_dma(ic1, h2, seg, ld),
                                            lambda h2=h2, seg=seg, ld=ld, ic1=ic + 1: emit_F2_unit(ic1, h2, seg, ld)))
            grs_of = [[HEAD_OF_POS[2 * p] // 4, HEAD_OF_POS[2 * p + 1] // 4] for p in range(8)]
            steps = [(p, jt) for p in range(8) for jt in range(jtlast + 1)]
            nf = len(f_units)
            ns = len(steps)
            ns_eff = max(1, ns - max(2, ns // 5))
            fpos = {"d": 0, "p": 0}

            def pop_f(target):
                while fpos["p"] < target and fpos["p"] < nf:
                    while fpos["d"] < min(fpos["p"] + 6, nf):
                        d = f_units[fpos["d"]][0]
                        if d is not None:
                            d()
                        fpos["d"] += 1
                    f_units[fpos["p"]][1]()
                    fpos["p"] += 1

            pend_tailB = None
            for si, (p, jt) in enumerate(steps):
                if jt >= 4:
                    emit_pv(ic, p, jt - 4, grs_of[p], jtlast)
                if jt == 2 and pend_tailB is not None:
                    emit_A_tailB(ic, pend_tailB, grs_of[pend_tailB], jtlast)
                    pend_tailB = None
                pop_f(nf * min(si + 1, ns_eff) // ns_eff)
                emit_A_step(ic, p, jt, grs_of[p], jtlast)
                if jt == jtlast:
                    emit_A_tailA(ic, p, grs_of[p], jtlast)
                    pend_tailB = p
            pop_f(nf)
            emit_A_tailB(ic, pend_tailB, grs_of[pend_tailB], jtlast)

        for t in range(TPC):
            emit_W_tile(NIC - 1, t)

        ctx.wait_only("gpsimd", [("ost", NIC - 1, TPC - 1)])

        if DEBUG_DUMPS:
            dwaits = [("ost", NIC - 1, TPC - 1), ("comb", NIC - 1, TPC - 1),
                      ("ot", NIC - 1, 7), ("gs", NIC - 1, 1, 1),
                      ("exp", NIC - 1, 7, TPC * NIC - 1), ("bias", NIC - 1)]
            for nm, sb in dump_list:
                dma("sync", ("dump", nm), dwaits,
                    lambda e, nm=nm, sb=sb: e.dma_start(dump_drams[nm], sb))

    events = {}
    rctx = Ctx(record=True, events=events)
    program(rctx)
    build_nc.last_counts = dict(rctx.counts)
    build_nc.last_rctx = rctx

    ksr = bass.get_kernel_semaphore_range()
    n_keep = 2 + (1 if nc._bir_kernel_barrier_sem is not None else 0) + len(nc._monotonic_sems)
    reset_range = range(ksr.start + n_keep, ksr.stop)
    with nc.Block() as blk0:
        @blk0.gpsimd
        def _(g):
            g.dma_reset(reset_range)
            g.sem_clear(reset_range)

    with nc.Block() as blk:
        @blk.sync
        def _(eng):
            program(Ctx(record=False, engine="sync", eng=eng, events=events, sems=sems))

        @blk.gpsimd
        def _(eng):
            program(Ctx(record=False, engine="gpsimd", eng=eng, events=events, sems=sems))

        @blk.tensor
        def _(eng):
            program(Ctx(record=False, engine="tensor", eng=eng, events=events, sems=sems))

        @blk.vector
        def _(eng):
            program(Ctx(record=False, engine="vector", eng=eng, events=events, sems=sems))

        @blk.scalar
        def _(eng):
            program(Ctx(record=False, engine="scalar", eng=eng, events=events, sems=sems))

    for g in reversed(sem_guards):
        g.__exit__(None, None, None)
    return nc


def make_in_map(cfg, x_core, Wqa, Wqb, Wkva, Wkvb, Wout, bout, wg, wu, wd):
    cos, sin, ident, ones, masks, perm_q, perm_k, perm_v = host_constants(cfg.S)
    return {
        "x": np.ascontiguousarray(x_core),
        "wqa": np.ascontiguousarray(Wqa),
        "wqb": np.ascontiguousarray(Wqb[:, perm_q]),
        "wkva": np.ascontiguousarray(Wkva),
        "wkvk": np.ascontiguousarray(Wkvb[:, perm_k]),
        "wkvv": np.ascontiguousarray(Wkvb[:, perm_v]),
        "woutp": np.ascontiguousarray(np.concatenate(
            [Wout[HEAD_OF_POS[q] * 64:(HEAD_OF_POS[q] + 1) * 64]
             for q in range(16)], axis=0)),
        "bout": np.ascontiguousarray(bout.reshape(1, E)),
        "wg": np.ascontiguousarray(wg),
        "wu": np.ascontiguousarray(wu),
        "wd": np.ascontiguousarray(wd),
        "cost": cos, "sint": sin, "ident": ident, "ones": ones, "masks": masks,
    }


def kernel(L0_attn_x, L0_attn_Wqa, L0_attn_Wqb, L0_attn_Wkva, L0_attn_Wkvb,
           L0_attn_Wout, L0_attn_bout, L0_ffn_w_gate, L0_ffn_w_up, L0_ffn_w_down):
    cfg = Cfg(S=4096, HID=3072)
    nc = build_nc(cfg)
    x = np.asarray(L0_attn_x)
    in_maps = [
        make_in_map(cfg, x[c], np.asarray(L0_attn_Wqa), np.asarray(L0_attn_Wqb),
                    np.asarray(L0_attn_Wkva), np.asarray(L0_attn_Wkvb),
                    np.asarray(L0_attn_Wout), np.asarray(L0_attn_bout),
                    np.asarray(L0_ffn_w_gate), np.asarray(L0_ffn_w_up),
                    np.asarray(L0_ffn_w_down))
        for c in range(8)
    ]
    res = run_bass_kernel_spmd(nc, in_maps, core_ids=list(range(8)))
    return np.stack([np.asarray(r["out"], dtype=np.float32) for r in res.results])
